# version 08
# baseline (speedup 1.0000x reference)
import contextlib
import os
import numpy as np
import concourse.bass as bass
import concourse.mybir as mybir
from concourse.bass_utils import run_bass_kernel_spmd
from concourse.aot_env import allow_ici_all_to_all

F32 = mybir.dt.float32
F16 = mybir.dt.float16
Alu = mybir.AluOpType
Act = mybir.ActivationFunctionType
AxX = mybir.AxisListType.X

B, L, D = 4, 8192, 1024
NF = 2 * L
DL = 128
NT = 4096
NC = 8
EPS_LN = 1e-5
EPS_GATE = 1e-6
BIG = 1e9

_CONSTS = None


def _consts():
    global _CONSTS
    if _CONSTS is not None:
        return _CONSTS
    n = np.arange(128)
    th = 2.0 * np.pi * np.outer(n[:64], n) / 128.0
    fc = np.zeros((128, 3, 128), np.float32)
    fc[:64, 0], fc[:64, 1], fc[:64, 2] = np.cos(th), -np.sin(th), np.sin(th)
    fc[64:] = fc[:64]
    th = 2.0 * np.pi * np.outer(n, n) / 128.0
    fpc = np.stack([np.cos(th), np.sin(th), -np.sin(th)], 1).astype(np.float32)
    k1g, n2g, k2g = n[:, None, None], n[None, :, None], n[None, None, :]
    th = 2.0 * np.pi * n2g * (k1g + 128 * k2g) / NF
    wt = np.stack([np.cos(th), -np.sin(th), np.sin(th)], 1).astype(np.float16)
    m1g, k1g2, m2g = n[:, None, None], n[None, :, None], np.arange(64)[None, None, :]
    th = 2.0 * np.pi * k1g2 * (m1g + 128 * m2g) / NF
    st = np.stack([np.cos(th), np.sin(th), -np.sin(th)], 1).astype(np.float16)
    ident = np.eye(128, dtype=np.float32)
    th2 = 2.0 * np.pi * np.outer(np.arange(64), np.arange(128)) / 128.0
    fs1 = np.zeros((128, 2, 128), np.float32)
    fs1[:64, 0], fs1[64:, 0] = np.cos(th2), np.sin(th2)
    fs1[:64, 1], fs1[64:, 1] = -np.sin(th2), np.cos(th2)
    _CONSTS = (fc, fpc, wt, st, ident, fs1)
    return _CONSTS


class Prog:
    def __init__(self):
        self.ops = {e: [] for e in ("sync", "gpsimd", "tensor", "vector", "scalar")}
        self.led = {}
        self.marks = {}

    def mark(self, name):
        self.marks[name] = {e: len(v) for e, v in self.ops.items()}

    def emit(self, eng, fn):
        self.ops[eng].append(fn)

    def inc(self, s, n=1):
        self.led[s] = self.led.get(s, 0) + n
        return self.led[s]

    def cnt(self, s):
        return self.led.get(s, 0)


def build_nc(trunc=None):
    nc = bass.Bass("TRN2", target_bir_lowering=False)

    d_uconv = nc.dram_tensor("u_conv", [B, 64, 128, DL], F32, kind="ExternalInput").ap()
    d_utok = nc.dram_tensor("u_tok", [NT, D], F32, kind="ExternalInput").ap()
    d_kconv = nc.dram_tensor("k_conv", [2, 64, 128, 64], F32, kind="ExternalInput").ap()
    d_fc = nc.dram_tensor("fc", [128, 3, 128], F32, kind="ExternalInput").ap()
    d_fs1 = nc.dram_tensor("fs1", [128, 2, 128], F32, kind="ExternalInput").ap()
    d_fpc = nc.dram_tensor("fpc", [128, 3, 128], F32, kind="ExternalInput").ap()
    d_wt = nc.dram_tensor("wt", [128, 3, 128, 128], F16, kind="ExternalInput").ap()
    d_st = nc.dram_tensor("st", [128, 3, 128, 64], F16, kind="ExternalInput").ap()
    d_idf = nc.dram_tensor("identf", [128, 128], F32, kind="ExternalInput").ap()
    d_wffn = nc.dram_tensor("wffn", [D, D], F32, kind="ExternalInput").ap()
    d_bias = nc.dram_tensor("biasv", [1, D], F32, kind="ExternalInput").ap()
    d_skip = nc.dram_tensor("skiprep", [128, D], F32, kind="ExternalInput").ap()
    d_rt = nc.dram_tensor("rt", [D, 8], F32, kind="ExternalInput").ap()
    d_at = nc.dram_tensor("at", [D, 128], F32, kind="ExternalInput").ap()
    d_bt2 = nc.dram_tensor("bt2", [128, D], F32, kind="ExternalInput").ap()
    d_out = nc.dram_tensor("out", [NT, D], F32, kind="ExternalOutput").ap()

    t_kdram = nc.dram_tensor("kdram", [128, 128, 256], F16)
    d_kdram = t_kdram.ap()
    t_agin = nc.dram_tensor("aginb", [1, 8192], F32)
    t_agout = nc.dram_tensor("agoutb", [8, 8192], F32)
    t_a2ain = nc.dram_tensor("a2ainb", [NC, NT, DL], F16)
    t_a2aout = nc.dram_tensor("a2aoutb", [NC, NT, DL], F16)
    d_agin, d_agout = t_agin.ap(), t_agout.ap()
    d_a2ain, d_a2aout = t_a2ain.ap(), t_a2aout.ap()

    P = Prog()

    with contextlib.ExitStack() as _st:
        EC = _st.enter_context
        fc_sb = EC(nc.sbuf_tensor("fc_sb", [128, 384], F16))
        fs1_sb = EC(nc.sbuf_tensor("fs1_sb", [128, 256], F16))
        fpc_sb = EC(nc.sbuf_tensor("fpc_sb", [128, 384], F16))
        id16 = EC(nc.sbuf_tensor("id16", [128, 128], F16))
        id32 = EC(nc.sbuf_tensor("id32", [128, 128], F32))
        ones1 = EC(nc.sbuf_tensor("ones1", [1, 128], F16))
        stats_sb = EC(nc.sbuf_tensor("stats", [128, 64], F32))
        statsall = EC(nc.sbuf_tensor("statsall", [128, 512], F32))
        ic0 = EC(nc.sbuf_tensor("ic0", [128, 128], F32))
        ic1 = EC(nc.sbuf_tensor("ic1", [128, 128], F32))
        gc0 = EC(nc.sbuf_tensor("gc0", [128, 128], F32))
        gc1 = EC(nc.sbuf_tensor("gc1", [128, 128], F32))
        wbuf = EC(nc.sbuf_tensor("wbuf", [128, 3072], F16))
        gt1 = EC(nc.sbuf_tensor("gt1", [128, 32], F32))
        gt2 = EC(nc.sbuf_tensor("gt2", [128, 32], F32))
        gt3 = EC(nc.sbuf_tensor("gt3", [128, 256], F32))
        pb0 = EC(nc.psum_tensor("pb0", [128, 512], F32))
        pb1 = EC(nc.psum_tensor("pb1", [128, 512], F32))
        pb2 = EC(nc.psum_tensor("pb2", [128, 512], F32))
        pb3 = EC(nc.psum_tensor("pb3", [128, 512], F32))
        pb4 = EC(nc.psum_tensor("pb4", [128, 512], F32))
        pb5 = EC(nc.psum_tensor("pb5", [128, 512], F32))
        pb6 = EC(nc.psum_tensor("pb6", [128, 512], F32))
        pb7 = EC(nc.psum_tensor("pb7", [128, 512], F32))
        sL = EC(nc.semaphore("sL"))
        sG = EC(nc.semaphore("sG"))
        sPE = EC(nc.semaphore("sPE"))
        sDV = EC(nc.semaphore("sDV"))
        sSC = EC(nc.semaphore("sSC"))
        sAG = EC(nc.semaphore("sAG"))
        sA2A = EC(nc.semaphore("sA2A"))
        sST = EC(nc.semaphore("sST"))
        sW0 = EC(nc.semaphore("sW0"))
        sW1 = EC(nc.semaphore("sW1"))
        sU0 = EC(nc.semaphore("sU0"))
        sU1 = EC(nc.semaphore("sU1"))
        sK0 = EC(nc.semaphore("sK0"))
        sK1 = EC(nc.semaphore("sK1"))
        sT0 = EC(nc.semaphore("sT0"))
        sT1 = EC(nc.semaphore("sT1"))
        sY0 = EC(nc.semaphore("sY0"))
        sY1 = EC(nc.semaphore("sY1"))
        sGC = EC(nc.semaphore("sGC"))
        sKD0 = EC(nc.semaphore("sKD0"))
        sKD1 = EC(nc.semaphore("sKD1"))
        sYS0 = EC(nc.semaphore("sYS0"))
        sYS1 = EC(nc.semaphore("sYS1"))
        sAC0 = EC(nc.semaphore("sAC0"))
        sAC1 = EC(nc.semaphore("sAC1"))
        sST1 = EC(nc.semaphore("sST1"))
        sW2 = EC(nc.semaphore("sW2"))
        block = EC(nc.Block())

        banks = [pb0, pb1, pb2, pb3, pb4, pb5, pb6, pb7]
        bank_guard = [None] * 8
        sems = dict(sL=sL, sG=sG, sPE=sPE, sDV=sDV, sSC=sSC, sAG=sAG,
                    sA2A=sA2A, sST=sST, sW0=sW0, sW1=sW1, sU0=sU0, sU1=sU1,
                    sK0=sK0, sK1=sK1, sT0=sT0, sT1=sT1, sY0=sY0, sY1=sY1,
                    sGC=sGC, sKD0=sKD0, sKD1=sKD1, sYS0=sYS0, sYS1=sYS1,
                    sAC0=sAC0, sAC1=sAC1, sST1=sST1, sW2=sW2)

        def W(eng, sname, val):
            if val <= 0:
                return
            s = sems[sname]
            P.emit(eng, lambda e, s=s, v=val: e.wait_ge(s, v))

        def g_dma(out_ap, in_ap, sname="sG"):
            s = sems[sname]
            P.emit("gpsimd", lambda e, o=out_ap, i=in_ap, s=s:
                   e.dma_start(out=o, in_=i).then_inc(s, 16))
            return P.inc(sname, 16)

        def s_dma(out_ap, in_ap, sname="sL"):
            s = sems[sname]
            P.emit("sync", lambda e, o=out_ap, i=in_ap, s=s:
                   e.dma_start(out=o, in_=i).then_inc(s, 16))
            return P.inc(sname, 16)

        def mm(out_ap, lhsT_ap, rhs_ap, start, stop, inc=None, tp=None, sgc=False):
            def f(e, o=out_ap, l=lhsT_ap, r=rhs_ap, st=start, sp=stop, inc=inc, tp=tp, sgc=sgc):
                i = e.matmul(o, l, r, start=st, stop=sp, tile_position=tp,
                             skip_group_check=sgc)
                if inc is not None:
                    i.then_inc(sems[inc], 1)
            P.emit("tensor", f)
            if inc is not None:
                return P.inc(inc)

        def v_op(fn, inc=False, drain=False):
            if inc:
                P.emit("vector", lambda e, f=fn: f(e).then_inc(sDV, 1))
                r = P.inc("sDV")
            else:
                P.emit("vector", fn)
                r = None
            if drain:
                P.emit("vector", lambda e: e.drain())
            return r

        def sc_op(fn, inc=False):
            if inc:
                P.emit("scalar", lambda e, f=fn: f(e).then_inc(sSC, 1))
                return P.inc("sSC")
            P.emit("scalar", fn)

        def bank_wait(bi):
            g = bank_guard[bi]
            if g is not None:
                W("tensor", g[0], g[1])
                bank_guard[bi] = None

        with contextlib.ExitStack() as _cs:
            ECC = _cs.enter_context
            rga = ECC(nc.sbuf_tensor("rga", [128, 32768], F16))
            rgb = ECC(nc.sbuf_tensor("rgb", [128, 32768], F16))
            ust = ECC(nc.sbuf_tensor("ust", [128, 2048], F32))
            tst = ECC(nc.sbuf_tensor("tst", [128, 1024], F32))
            kstg = ECC(nc.sbuf_tensor("kstg", [128, 2048], F16))
            kpw = ECC(nc.sbuf_tensor("kpw", [128, 2048], F16))
            ptmp = ECC(nc.sbuf_tensor("ptmp", [128, 2048], F16))
            ystg = ECC(nc.sbuf_tensor("ystg", [128, 4096], F16))

            _es = contextlib.ExitStack()
            k_sb = _es.enter_context(nc.sbuf_tensor("k_sb", [128, 8192], F16, side="right"))
            utst = _es.enter_context(nc.sbuf_tensor("utst", [128, 2048], F32, side="right"))
            sqtmp = _es.enter_context(nc.sbuf_tensor("sqtmp", [128, 1024], F32, side="right"))

            kc = d_kconv.rearrange("dh n1 n2 dl -> (dh n1) n2 dl")
            g_dma(k_sb[0:64, :].rearrange("p (n2 dl) -> p n2 dl", dl=64), kc[0:64], "sGC")
            g_dma(k_sb[64:128, :].rearrange("p (n2 dl) -> p n2 dl", dl=64), kc[64:128], "sGC")
            g_dma(fc_sb[:, :].rearrange("p (a b) -> p a b", a=3), d_fc, "sGC")
            g_dma(fs1_sb[:, :].rearrange("p (a b) -> p a b", a=2), d_fs1, "sGC")
            g_dma(fpc_sb[:, :].rearrange("p (a b) -> p a b", a=3), d_fpc, "sGC")
            g_dma(id16[:, :], d_idf, "sGC")
            ld_k = P.cnt("sGC")
            s_dma(id32[:, :], d_idf)
            ld_id32 = P.cnt("sL")
            P.emit("vector", lambda e: e.memset(ones1[:, :], 1.0))
            P.emit("vector", lambda e: e.drain())

            P.mark('p0')
            tok_ss = ["sT0", "sT1"]
            red_done = {}
            for tc in range(32):
                sl = tc % 2
                if tc >= 2:
                    W("sync", "sDV", red_done[tc - 2])
                s_dma(utst[:, sl * 1024:(sl + 1) * 1024],
                      d_utok[tc * 128:(tc + 1) * 128, :], tok_ss[sl])
                W("scalar", tok_ss[sl], P.cnt(tok_ss[sl]))
                if tc >= 1:
                    W("scalar", "sDV", red_done[tc - 1])
                sq = sc_op(lambda e, sl=sl: e.activation(
                    out=sqtmp[:, :], in_=utst[:, sl * 1024:(sl + 1) * 1024],
                    func=Act.Square), inc=True)
                W("vector", tok_ss[sl], P.cnt(tok_ss[sl]))
                v_op(lambda e, sl=sl, tc=tc: e.reduce_sum(
                    stats_sb[:, tc:tc + 1], utst[:, sl * 1024:(sl + 1) * 1024], axis=AxX))
                W("vector", "sSC", sq)
                red_done[tc] = v_op(lambda e, tc=tc: e.reduce_sum(
                    stats_sb[:, 32 + tc:32 + tc + 1], sqtmp[:, :], axis=AxX), inc=True)
            P.emit("vector", lambda e: e.drain())
            v_op(lambda e: e.tensor_scalar(gt1[:, :], stats_sb[:, 0:32], 1.0 / D, None, Alu.mult), drain=True)
            v_op(lambda e: e.tensor_scalar(gt2[:, :], stats_sb[:, 32:64], 1.0 / D, None, Alu.mult), drain=True)
            v_op(lambda e: e.tensor_tensor(gt3[:, 0:32], gt1[:, :], gt1[:, :], Alu.mult), drain=True)
            v_op(lambda e: e.tensor_tensor(gt2[:, :], gt2[:, :], gt3[:, 0:32], Alu.subtract), drain=True)
            v_op(lambda e: e.tensor_scalar(gt2[:, :], gt2[:, :], EPS_LN, None, Alu.add), drain=True)
            rec = v_op(lambda e: e.reciprocal(gt3[:, 0:32], gt2[:, :]), inc=True, drain=True)
            W("scalar", "sDV", rec)
            inv_d = sc_op(lambda e: e.activation(out=stats_sb[:, 0:32], in_=gt3[:, 0:32],
                                                 func=Act.Sqrt), inc=True)
            W("vector", "sSC", inv_d)
            v_op(lambda e: e.tensor_tensor(gt2[:, :], gt1[:, :], stats_sb[:, 0:32], Alu.mult), drain=True)
            stats_fin = v_op(lambda e: e.tensor_scalar(
                stats_sb[:, 32:64], gt2[:, :], -1.0, None, Alu.mult), inc=True, drain=True)

            W("gpsimd", "sDV", stats_fin)
            g_dma(d_agin.rearrange("o (p c) -> (o p) c", p=128), stats_sb[:, :])
            W("gpsimd", "sG", P.cnt("sG"))
            P.emit("gpsimd", lambda e: e.collective_compute(
                "AllGather", Alu.bypass, replica_groups=[list(range(NC))],
                ins=[t_agin.ap().opt()], outs=[t_agout.ap().opt()]).then_inc(sAG, 1))
            P.inc("sAG")
            W("gpsimd", "sAG", 1)
            g_dma(statsall[:, :].rearrange("p (c w) -> p c w", c=8),
                  d_agout.rearrange("c (p w) -> p c w", p=128))
            ld_statsall = P.cnt("sG")

            P.mark('stats')
            W("tensor", "sG", ld_statsall)
            W("tensor", "sL", ld_id32)
            for p2 in range(2):
                for ty in range(2):
                    bi = 2 * p2 + ty
                    bank_wait(bi)
                    pk = banks[bi]
                    last = None
                    for b2 in range(2):
                        for h in range(2):
                            q = 2 * b2 + h
                            co = (2 * (2 * p2 + b2) + h) * 64 + ty * 32
                            last = mm(pk[32 * q:32 * q + 32, 0:128],
                                      statsall[:, co:co + 32], id32[:, :],
                                      True, True,
                                      inc="sPE" if q == 3 else None,
                                      tp=(0, 32 * q))
                    dst = (ic0, ic1)[p2] if ty == 0 else (gc0, gc1)[p2]
                    W("vector", "sPE", last)
                    v_op(lambda e, dst=dst, pk=pk: e.tensor_copy(dst[:, :], pk[:, 0:128]),
                         inc=True)
                    bank_guard[bi] = ("sDV", P.cnt("sDV"))
            cstats_done = P.cnt("sDV")
            P.emit("vector", lambda e: e.drain())

            P.mark('statsT')
            W("tensor", "sGC", ld_k)
            ks1_last = None
            for d in range(DL):
                dh, dl = d // 64, d % 64
                bi = (d // 2) % 8
                half = d % 2
                if half == 0:
                    bank_wait(bi)
                lhs = k_sb[dh * 64:(dh + 1) * 64, dl:8192:64]
                ftr = fc_sb[dh * 64:(dh + 1) * 64, 0:128]
                fti = fc_sb[dh * 64:(dh + 1) * 64, 128:256]
                tp = (64, 0) if dh else None
                pk = banks[bi]
                mm(pk[:, half * 256:half * 256 + 128], lhs, ftr, True, True, tp=tp)
                last = mm(pk[:, half * 256 + 128:half * 256 + 256], lhs, fti, True, True,
                          inc="sPE" if half == 1 else None, tp=tp)
                if half == 1:
                    ks1_last = last
                    eng = "vector" if (d // 2) % 2 == 0 else "scalar"
                    W(eng, "sPE", last)
                    dst = rgb[:, (d - 1) * 256:(d + 1) * 256]
                    if eng == "vector":
                        v_op(lambda e, dst=dst, pk=pk: e.tensor_copy(dst, pk[:, :]), inc=True)
                        bank_guard[bi] = ("sDV", P.cnt("sDV"))
                    else:
                        sc_op(lambda e, dst=dst, pk=pk: e.copy(dst, pk[:, :]), inc=True)
                        bank_guard[bi] = ("sSC", P.cnt("sSC"))

            P.mark('ks1')
            ak_v, ak_s = P.cnt("sDV"), P.cnt("sSC")
            wss = ["sW0", "sW1"]
            pe_blk = {}
            kdram_cnt = {}
            for kb in range(32):
                sl = kb % 2
                if kb >= 2:
                    W("sync", "sPE", pe_blk[kb - 2])
                    W("scalar", "sKD%d" % (kb % 2), kdram_cnt[kb - 2])
                s_dma(wbuf[:, sl * 1536:(sl + 1) * 1536].rearrange(
                    "p (a b c) -> p a b c", a=4, b=3),
                    d_wt[kb * 4:(kb + 1) * 4].rearrange("a b p c -> p a b c"), wss[sl])
                W("tensor", wss[sl], P.cnt(wss[sl]))
                if kb == 0:
                    W("tensor", "sDV", ak_v)
                    W("tensor", "sSC", ak_s)
                for j in range(4):
                    k1 = kb * 4 + j
                    base = sl * 1536 + j * 384
                    wr = wbuf[:, base:base + 128]
                    wi = wbuf[:, base + 128:base + 256]
                    wn = wbuf[:, base + 256:base + 384]
                    bi = 4 + (k1 % 2)
                    bank_wait(bi)
                    pk = banks[bi]
                    akr = rgb[:, k1:32768:256]
                    aki = rgb[:, 128 + k1:32768:256]
                    mm(pk[:, 0:128], wr, akr, True, False)
                    mm(pk[:, 0:128], wn, aki, False, True)
                    mm(pk[:, 128:256], wi, akr, True, False)
                    mk = mm(pk[:, 128:256], wr, aki, False, True, inc="sPE")
                    pe_blk[kb] = mk
                    W("scalar", "sPE", mk)
                    sc_op(lambda e, pk=pk, sl=sl, j=j: e.activation(
                        out=kstg[:, sl * 1024 + j * 256: sl * 1024 + (j + 1) * 256],
                        in_=pk[:, 0:256], func=Act.Copy, scale=1.0 / NF), inc=True)
                    bank_guard[bi] = ("sSC", P.cnt("sSC"))
                W("gpsimd", "sSC", P.cnt("sSC"))
                kdram_cnt[kb] = g_dma(
                    d_kdram[kb * 4:(kb + 1) * 4].rearrange("a p c -> p a c"),
                    kstg[:, sl * 1024:(sl + 1) * 1024].rearrange("p (a c) -> p a c", a=4),
                    "sKD%d" % (kb % 2))
            kdram_done = (P.cnt("sKD0"), P.cnt("sKD1"))

            P.mark('ks2')
            _dbg0 = os.environ.get("S1X", "")
            if "M" in _dbg0:
                for _q in range(4):
                    mm(banks[0][:, _q * 128:(_q + 1) * 128], fpc_sb[:, 0:128],
                       fpc_sb[:, 128:256], True, True)
                P.mark('premm')
            _es.close()
            rxn = ECC(nc.sbuf_tensor("rxn", [128, 16384], F16))
            if "N" in _dbg0:
                for _q in range(4):
                    mm(banks[0][:, _q * 128:(_q + 1) * 128], fpc_sb[:, 0:128],
                       fpc_sb[:, 128:256], True, True)
                P.mark('postmm')

            uss = ["sU0", "sU1"]
            kss = ["sK0", "sK1"]
            xn_chunk = {}
            s1_last = {}
            is2_blk = {}
            ystg_dma = {}
            pw_chunk = {}
            z_last = None
            for pr in range(2):
                for ck in range(16):
                    if pr == 0:
                        P.mark('p1ck%d' % ck)
                    sl = ck % 2
                    if pr > 0 or ck >= 2:
                        W("sync", "sDV", xn_chunk[(pr, ck - 2)] if ck >= 2
                          else xn_chunk[(pr - 1, 14 + ck)])
                    if pr > 0 and ck < 2:
                        W("sync", "sPE", s1_last[pr - 1])
                    if "R" in os.environ.get("S1X", ""):
                        src = d_utok[0:128, 0:1024]
                        s_dma(ust[:, sl * 1024:(sl + 1) * 1024], src, uss[sl])
                    else:
                        src = d_uconv[2 * pr:2 * pr + 2, :, ck * 8:(ck + 1) * 8, :].rearrange(
                            "b n1 n2 d -> (b n1) n2 d")
                        s_dma(ust[:, sl * 1024:(sl + 1) * 1024].rearrange(
                            "p (n2 d) -> p n2 d", d=128), src, uss[sl])
                    W("vector", uss[sl], P.cnt(uss[sl]))
                    if pr == 0 and ck < 2:
                        W("vector", "sDV", cstats_done)
                        W("vector", "sPE", pe_blk[1] if ck == 0 else pe_blk[1])
                    if pr > 0 and ck < 2:
                        W("vector", "sPE", s1_last[pr - 1])
                    iv, gv = (ic0, ic1)[pr], (gc0, gc1)[pr]
                    _dbgP = os.environ.get("S1X", "")
                    if "P" in _dbgP:
                        xn_chunk[(pr, ck)] = v_op(lambda e, sl=sl, ck=ck: e.tensor_copy(
                            rxn[:, ck * 1024:(ck + 1) * 1024],
                            ust[:, sl * 1024:(sl + 1) * 1024]), inc=True, drain=True)
                    else:
                        v_op(lambda e, sl=sl, iv=iv, ck=ck: e.tensor_tensor(
                            tst[:, :].rearrange("p (n d) -> p n d", d=128),
                            ust[:, sl * 1024:(sl + 1) * 1024].rearrange("p (n d) -> p n d", d=128),
                            iv[:, ck * 8:(ck + 1) * 8].unsqueeze(2).broadcast_to([128, 8, 128]),
                            Alu.mult), drain=True)
                        xn_chunk[(pr, ck)] = v_op(lambda e, gv=gv, ck=ck: e.tensor_tensor(
                            rxn[:, ck * 1024:(ck + 1) * 1024].rearrange("p (n d) -> p n d", d=128),
                            tst[:, :].rearrange("p (n d) -> p n d", d=128),
                            gv[:, ck * 8:(ck + 1) * 8].unsqueeze(2).broadcast_to([128, 8, 128]),
                            Alu.add), inc=True, drain=True)
                xn_all = xn_chunk[(pr, 15)]
                P.mark('xn%d' % pr)
                if pr == 0 and "T" in os.environ.get("S1X", ""):
                    for _q in range(4):
                        mm(banks[0][:, _q * 128:(_q + 1) * 128], fpc_sb[:, 0:128],
                           fpc_sb[:, 128:256], True, True)
                    P.mark('xnmm')

                if "C" not in os.environ.get("S1X", ""):
                    W("tensor", "sDV", xn_all)
                for dl in range(DL):
                    bi = (dl // 2) % 8
                    half = dl % 2
                    if half == 0:
                        bank_wait(bi)
                    zz = rxn[:, dl:16384:128]
                    pk = banks[bi]
                    o_re = pk[:, half * 256:half * 256 + 128]
                    o_im = pk[:, half * 256 + 128:half * 256 + 256]
                    mm(o_re, zz, fs1_sb[:, 0:128], True, True)
                    last = mm(o_im, zz, fs1_sb[:, 128:256], True, True, inc="sPE")
                    if dl in (1, 15, 63):
                        P.mark('s1_%d_dl%d' % (pr, dl))
                    if half == 1:
                        s1_last[pr] = last
                        eng = "vector" if (dl // 2) % 2 == 0 else "scalar"
                        W(eng, "sPE", last)
                        dst = rga[:, (dl - 1) * 256:(dl + 1) * 256]
                        if eng == "vector":
                            v_op(lambda e, dst=dst, pk=pk: e.tensor_copy(dst, pk[:, :]), inc=True)
                            bank_guard[bi] = ("sDV", P.cnt("sDV"))
                        else:
                            sc_op(lambda e, dst=dst, pk=pk: e.copy(dst, pk[:, :]), inc=True)
                            bank_guard[bi] = ("sSC", P.cnt("sSC"))
                at_v, at_s = P.cnt("sDV"), P.cnt("sSC")
                P.mark('s1_%d' % pr)

                pe_blk2 = {}
                wslot3 = [wbuf[:, 0:1536], wbuf[:, 1536:3072], kstg[:, 0:1536]]
                wss3 = ["sW0", "sW1", "sW2"]
                for kb in range(32):
                    sl = kb % 3
                    W("sync", "sPE", pe_blk2[kb - 3] if kb >= 3 else
                      (pe_blk[31] if pr == 0 else is2_blk[(0, 15)]))
                    if pr == 0 and kb == 2:
                        W("sync", "sKD0", kdram_done[0])
                        W("sync", "sKD1", kdram_done[1])
                    s_dma(wslot3[sl].rearrange(
                        "p (a b c) -> p a b c", a=4, b=3),
                        d_wt[kb * 4:(kb + 1) * 4].rearrange("a b p c -> p a b c"), wss3[sl])
                    W("tensor", wss3[sl], P.cnt(wss3[sl]))
                    if kb == 0:
                        W("tensor", "sDV", at_v)
                        W("tensor", "sSC", at_s)
                    for j in range(4):
                        k1 = kb * 4 + j
                        base = j * 384
                        wr = wslot3[sl][:, base:base + 128]
                        wi = wslot3[sl][:, base + 128:base + 256]
                        wn = wslot3[sl][:, base + 256:base + 384]
                        bi = (k1 // 2) % 2
                        qq = k1 % 2
                        if qq == 0:
                            bank_wait(bi)
                        pk = banks[bi]
                        ar = rga[:, k1:32768:256]
                        ai = rga[:, 128 + k1:32768:256]
                        o_re = pk[:, qq * 256:qq * 256 + 128]
                        o_im = pk[:, qq * 256 + 128:qq * 256 + 256]
                        mm(o_re, wr, ar, True, False)
                        mm(o_re, wn, ai, False, True)
                        mm(o_im, wi, ar, True, False)
                        mz = mm(o_im, wr, ai, False, True, inc="sPE")
                        pe_blk2[kb] = mz
                        z_last = mz
                        if qq == 1:
                            eng = "vector" if (k1 // 2) % 2 == 0 else "scalar"
                            W(eng, "sPE", mz)
                            dst = rgb[:, (k1 - 1) * 256:(k1 + 1) * 256]
                            if eng == "vector":
                                v_op(lambda e, dst=dst, pk=pk: e.tensor_copy(dst, pk[:, :]), inc=True)
                                bank_guard[bi] = ("sDV", P.cnt("sDV"))
                            else:
                                sc_op(lambda e, dst=dst, pk=pk: e.copy(dst, pk[:, :]), inc=True)
                                bank_guard[bi] = ("sSC", P.cnt("sSC"))
                z_v, z_s = P.cnt("sDV"), P.cnt("sSC")
                P.mark('s2_%d' % pr)

                if pr == 0:
                    W("sync", "sKD0", kdram_done[0])
                    W("sync", "sKD1", kdram_done[1])
                for kc2 in range(32):
                    sl = kc2 % 2
                    if pr > 0 or kc2 >= 2:
                        W("sync", "sDV", pw_chunk[(pr, kc2 - 2)] if kc2 >= 2
                          else pw_chunk[(pr - 1, 30 + kc2)])
                    s_dma(kpw[:, sl * 1024:(sl + 1) * 1024].rearrange(
                        "p (a c) -> p a c", a=4),
                        d_kdram[kc2 * 4:(kc2 + 1) * 4].rearrange("a p c -> p a c"), kss[sl])
                    W("vector", kss[sl], P.cnt(kss[sl]))
                    if kc2 < 2:
                        W("vector", "sDV", z_v)
                        W("vector", "sSC", z_s)
                    zre = rgb[:, kc2 * 1024:(kc2 + 1) * 1024].rearrange(
                        "p (k a) -> p k a", a=256)[:, :, 0:128]
                    zim = rgb[:, kc2 * 1024:(kc2 + 1) * 1024].rearrange(
                        "p (k a) -> p k a", a=256)[:, :, 128:256]
                    kre = kpw[:, sl * 1024:(sl + 1) * 1024].rearrange(
                        "p (k a) -> p k a", a=256)[:, :, 0:128]
                    kim = kpw[:, sl * 1024:(sl + 1) * 1024].rearrange(
                        "p (k a) -> p k a", a=256)[:, :, 128:256]
                    t1 = ptmp[:, 0:512].rearrange("p (k a) -> p k a", a=128)
                    t2 = ptmp[:, 512:1024].rearrange("p (k a) -> p k a", a=128)
                    t3 = ptmp[:, 1024:1536].rearrange("p (k a) -> p k a", a=128)
                    t4 = ptmp[:, 1536:2048].rearrange("p (k a) -> p k a", a=128)
                    wre = rga[:, kc2 * 1024:(kc2 + 1) * 1024].rearrange(
                        "p (k a) -> p k a", a=256)[:, :, 0:128]
                    wim = rga[:, kc2 * 1024:(kc2 + 1) * 1024].rearrange(
                        "p (k a) -> p k a", a=256)[:, :, 128:256]
                    v_op(lambda e, a=t1, b=zre, c=kre: e.tensor_tensor(a, b, c, Alu.mult))
                    v_op(lambda e, a=t2, b=zim, c=kim: e.tensor_tensor(a, b, c, Alu.mult))
                    v_op(lambda e, a=t3, b=zre, c=kim: e.tensor_tensor(a, b, c, Alu.mult))
                    v_op(lambda e, a=t4, b=zim, c=kre: e.tensor_tensor(a, b, c, Alu.mult),
                         drain=True)
                    v_op(lambda e, a=wre, b=t1, c=t2: e.tensor_tensor(a, b, c, Alu.subtract))
                    pw_chunk[(pr, kc2)] = v_op(lambda e, a=wim, b=t3, c=t4: e.tensor_tensor(
                        a, b, c, Alu.add), inc=True, drain=True)
                pw_all = pw_chunk[(pr, 31)]
                P.mark('pw_%d' % pr)

                W("tensor", "sDV", pw_all)
                for dl in range(DL):
                    bi = (dl // 2) % 8
                    half = dl % 2
                    if half == 0:
                        bank_wait(bi)
                    wre = rga[:, dl:32768:256]
                    wim = rga[:, 128 + dl:32768:256]
                    pk = banks[bi]
                    o_re = pk[:, half * 256:half * 256 + 128]
                    o_im = pk[:, half * 256 + 128:half * 256 + 256]
                    mm(o_re, wre, fpc_sb[:, 0:128], True, False)
                    mm(o_re, wim, fpc_sb[:, 256:384], False, True)
                    mm(o_im, wre, fpc_sb[:, 128:256], True, False)
                    last = mm(o_im, wim, fpc_sb[:, 0:128], False, True, inc="sPE")
                    if half == 1:
                        eng = "vector" if (dl // 2) % 2 == 0 else "scalar"
                        W(eng, "sPE", last)
                        dst = rgb[:, (dl - 1) * 256:(dl + 1) * 256]
                        if eng == "vector":
                            v_op(lambda e, dst=dst, pk=pk: e.tensor_copy(dst, pk[:, :]), inc=True)
                            bank_guard[bi] = ("sDV", P.cnt("sDV"))
                        else:
                            sc_op(lambda e, dst=dst, pk=pk: e.copy(dst, pk[:, :]), inc=True)
                            bank_guard[bi] = ("sSC", P.cnt("sSC"))
                bt_v, bt_s = P.cnt("sDV"), P.cnt("sSC")
                P.mark('is1_%d' % pr)

                for mb in range(16):
                    sl = mb % 2
                    W("sync", "sPE", is2_blk[(pr, mb - 2)] if mb >= 2 else z_last)
                    s_dma(wbuf[:, sl * 1536:(sl + 1) * 1536].rearrange(
                        "p (a b c) -> p a b c", a=8, b=3),
                        d_st[mb * 8:(mb + 1) * 8].rearrange("a b p c -> p a b c"), wss[sl])
                    W("tensor", wss[sl], P.cnt(wss[sl]))
                    if mb == 0:
                        W("tensor", "sDV", bt_v)
                        W("tensor", "sSC", bt_s)
                    if pr > 0 or mb >= 2:
                        pv = ystg_dma[(pr, mb - 2)] if mb >= 2 else ystg_dma[(pr - 1, 14 + mb)]
                        W("vector", "sYS%d" % (mb % 2), pv)
                    for j in range(8):
                        m1 = mb * 8 + j
                        bi = (m1 // 2) % 4
                        qq = m1 % 2
                        if qq == 0:
                            bank_wait(bi)
                        base = sl * 1536 + j * 192
                        sr = wbuf[:, base:base + 64]
                        si = wbuf[:, base + 64:base + 128]
                        sn = wbuf[:, base + 128:base + 192]
                        btr = rgb[:, m1:32768:256]
                        bti = rgb[:, 128 + m1:32768:256]
                        pk = banks[bi]
                        o_re = pk[0:64, qq * 256:qq * 256 + 128]
                        o_im = pk[0:64, qq * 256 + 128:qq * 256 + 256]
                        mm(o_re, sr, btr, True, False)
                        mm(o_re, sn, bti, False, True)
                        mm(o_im, si, btr, True, False)
                        lastm = mm(o_im, sr, bti, False, True, inc="sPE")
                        is2_blk[(pr, mb)] = lastm
                        if qq == 1:
                            W("vector", "sPE", lastm)
                            ybase = sl * 2048 + (j - 1) * 256
                            v_op(lambda e, pk=pk, ybase=ybase: e.tensor_copy(
                                ystg[0:64, ybase:ybase + 512], pk[0:64, 0:512]), inc=True)
                            bank_guard[bi] = ("sDV", P.cnt("sDV"))
                    W("gpsimd", "sDV", P.cnt("sDV"))
                    for b2 in range(2):
                        for h in range(2):
                            jd = (2 * pr + b2) * 2 + h
                            src = ystg[h * 32:h * 32 + 32,
                                       sl * 2048:(sl + 1) * 2048].rearrange(
                                       "p (a c b) -> p a c b", a=8, c=2)[
                                       :, :, b2:b2 + 1, :].squeeze(2)
                            dst = d_a2ain[jd].rearrange(
                                "(m2 m1) d -> m2 m1 d", m1=128)[:, mb * 8:(mb + 1) * 8, :]
                            g_dma(dst, src, "sYS%d" % (mb % 2))
                    ystg_dma[(pr, mb)] = P.cnt("sYS%d" % (mb % 2))
                P.mark('is2_%d' % pr)

            P.mark('pairs')
            W("gpsimd", "sYS0", P.cnt("sYS0"))
            W("gpsimd", "sYS1", P.cnt("sYS1"))

            def _do_a2a(e):
                with allow_ici_all_to_all():
                    e.collective_compute(
                        "AllToAll", Alu.bypass, replica_groups=[list(range(NC))],
                        ins=[t_a2ain.ap().opt()],
                        outs=[t_a2aout.ap().opt()]).then_inc(sA2A, 1)
            P.emit("gpsimd", _do_a2a)
            P.inc("sA2A")
            conv_pe_done = P.cnt("sPE")

        P.mark('a2a')
        with contextlib.ExitStack() as _fs:
            ECF = _fs.enter_context
            xdt = ECF(nc.sbuf_tensor("xdt", [128, 32768], F16))
            ysk = ECF(nc.sbuf_tensor("ysk", [128, 32768], F16))
            wffn_sb = ECF(nc.sbuf_tensor("wffn_sb", [128, 8192], F16))
            at_sb = ECF(nc.sbuf_tensor("at_sb", [128, 1024], F16))
            rt_sb = ECF(nc.sbuf_tensor("rt_sb", [128, 64], F16))
            bt2_sb = ECF(nc.sbuf_tensor("bt2_sb", [128, 1024], F16))
            bias_sb = ECF(nc.sbuf_tensor("bias_sb", [1, 1024], F16))
            skrep = ECF(nc.sbuf_tensor("skrep", [128, 1024], F16))
            xtn = ECF(nc.sbuf_tensor("xtn", [128, 2048], F16))
            h_sb = ECF(nc.sbuf_tensor("h_sb", [128, 4096], F16))
            ghT = ECF(nc.sbuf_tensor("ghT", [128, 4096], F16))
            lg_sb = ECF(nc.sbuf_tensor("lg_sb", [128, 256], F32))
            gat = ECF(nc.sbuf_tensor("gat", [128, 256], F32))
            gat16 = ECF(nc.sbuf_tensor("gat16", [128, 256], F16))
            y_sb = ECF(nc.sbuf_tensor("y_sb", [128, 2048], F16))
            ypl = ECF(nc.sbuf_tensor("ypl", [128, 2048], F16))
            osb = ECF(nc.sbuf_tensor("osb", [128, 2048], F32))
            ut2 = ypl[:, 0:2048].bitcast(F32)
            gt4 = osb[:, 0:256]
            gt5 = lg_sb

            g_dma(wffn_sb[:, :].rearrange("p (c o) -> p c o", c=8),
                  d_wffn.rearrange("(c p) o -> p c o", c=8))
            g_dma(at_sb[:, :].rearrange("p (c o) -> p c o", c=8),
                  d_at.rearrange("(c p) o -> p c o", c=8))
            g_dma(rt_sb[:, :].rearrange("p (c o) -> p c o", c=8),
                  d_rt.rearrange("(c p) o -> p c o", c=8))
            g_dma(bt2_sb[:, :], d_bt2)
            g_dma(bias_sb[:, :], d_bias)
            g_dma(skrep[:, :], d_skip)
            ld_ffn = P.cnt("sG")

            xdt_done = {}
            nrm_done = {}
            for tc in range(32):
                sl = tc % 2
                if tc == 0:
                    W("sync", "sPE", conv_pe_done)
                if tc >= 1:
                    W("sync", "sDV", nrm_done[tc - 1])
                s_dma(ut2,
                      d_utok[tc * 128:(tc + 1) * 128, :], ["sT0", "sT1"][sl])
                W("vector", ["sT0", "sT1"][sl], P.cnt(["sT0", "sT1"][sl]))
                if tc == 0:
                    W("vector", "sG", ld_ffn)
                    W("vector", "sPE", conv_pe_done)
                    W("vector", "sYS0", P.cnt("sYS0"))
                    W("vector", "sYS1", P.cnt("sYS1"))
                v_op(lambda e, sl=sl, tc=tc: e.tensor_scalar(
                    xtn[:, sl * 1024:(sl + 1) * 1024],
                    ut2,
                    stats_sb[:, tc:tc + 1], stats_sb[:, 32 + tc:32 + tc + 1],
                    Alu.mult, Alu.add), drain=True)
                nrm_done[tc] = v_op(lambda e, sl=sl, tc=tc: e.tensor_tensor(
                    ysk[:, tc * 1024:(tc + 1) * 1024],
                    xtn[:, sl * 1024:(sl + 1) * 1024], skrep[:, :],
                    Alu.mult), inc=True, drain=True)
                for dc in range(8):
                    bi = 4 + ((tc * 8 + dc) % 4)
                    bank_wait(bi)
                    if dc == 0:
                        W("tensor", "sDV", nrm_done[tc])
                    pk = banks[bi]
                    last = mm(pk[:, 0:128],
                              xtn[:, sl * 1024 + dc * 128: sl * 1024 + (dc + 1) * 128],
                              id16[:, :], True, True, inc="sPE")
                    eng = "vector" if dc % 2 == 0 else "scalar"
                    W(eng, "sPE", last)
                    dst = xdt[:, dc * 4096 + tc * 128: dc * 4096 + (tc + 1) * 128]
                    if eng == "vector":
                        v_op(lambda e, dst=dst, pk=pk: e.tensor_copy(dst, pk[:, 0:128]), inc=True)
                        bank_guard[bi] = ("sDV", P.cnt("sDV"))
                    else:
                        sc_op(lambda e, dst=dst, pk=pk: e.copy(dst, pk[:, 0:128]), inc=True)
                        bank_guard[bi] = ("sSC", P.cnt("sSC"))
                xdt_done[tc] = ("sDV", P.cnt("sDV"), "sSC", P.cnt("sSC"))

            hl_done = {}
            for tc in range(32):
                bi = 6 + (tc % 2)
                bank_wait(bi)
                dvn, dvv, scn, scv = xdt_done[tc]
                W("tensor", dvn, dvv)
                W("tensor", scn, scv)
                pk = banks[bi]
                lastp = None
                for dc in range(8):
                    lhs = xdt[:, dc * 4096 + tc * 128: dc * 4096 + (tc + 1) * 128]
                    mm(pk[:, 0:128], lhs, at_sb[:, dc * 128:(dc + 1) * 128], dc == 0, dc == 7)
                for dc in range(8):
                    lhs = xdt[:, dc * 4096 + tc * 128: dc * 4096 + (tc + 1) * 128]
                    lastp = mm(pk[:, 128:136], lhs, rt_sb[:, dc * 8:(dc + 1) * 8],
                               dc == 0, dc == 7, inc="sPE" if dc == 7 else None)
                W("vector", "sPE", lastp)
                v_op(lambda e, tc=tc, pk=pk: e.tensor_copy(
                    h_sb[:, tc * 128:(tc + 1) * 128], pk[:, 0:128]))
                hl_done[tc] = v_op(lambda e, tc=tc, pk=pk: e.tensor_copy(
                    lg_sb[:, tc * 8:(tc + 1) * 8], pk[:, 128:136]), inc=True)
                bank_guard[bi] = ("sDV", hl_done[tc])
            P.emit("vector", lambda e: e.drain())

            lgv = lg_sb[:, :].rearrange("p (t e) -> p t e", e=8)
            W("vector", "sDV", hl_done[31])
            v_op(lambda e, lgv=lgv: e.reduce_max(gt1[:, :], lgv, axis=AxX, negate=True),
                 drain=True)
            v_op(lambda e, lgv=lgv: e.tensor_tensor(
                gt4.rearrange("p (t e) -> p t e", e=8), lgv,
                gt1[:, :].unsqueeze(2).broadcast_to([128, 32, 8]), Alu.add), drain=True)
            v_op(lambda e: e.tensor_scalar(gat[:, :], gt4[:, :], 0.0, None, Alu.is_ge),
                 drain=True)
            v_op(lambda e: e.scalar_tensor_tensor(
                gt3[:, :], gat[:, :], -BIG, lg_sb[:, :], Alu.mult, Alu.add), drain=True)
            v_op(lambda e: e.reduce_max(
                gt2[:, :], gt3[:, :].rearrange("p (t e) -> p t e", e=8), axis=AxX,
                negate=True), drain=True)
            v_op(lambda e: e.tensor_tensor(
                gt3[:, :].rearrange("p (t e) -> p t e", e=8),
                gt3[:, :].rearrange("p (t e) -> p t e", e=8),
                gt2[:, :].unsqueeze(2).broadcast_to([128, 32, 8]), Alu.add), drain=True)
            v_op(lambda e: e.tensor_scalar(gt3[:, :], gt3[:, :], 0.0, None, Alu.is_ge),
                 drain=True)
            mask_d = v_op(lambda e: e.tensor_tensor(gat[:, :], gat[:, :], gt3[:, :], Alu.add),
                          inc=True, drain=True)
            W("scalar", "sDV", mask_d)
            exp_d = sc_op(lambda e: e.activation(out=gt5[:, :], in_=gt4, func=Act.Exp),
                          inc=True)
            W("vector", "sSC", exp_d)
            v_op(lambda e: e.reduce_sum(
                gt1[:, :], gt5[:, :].rearrange("p (t e) -> p t e", e=8), axis=AxX), drain=True)
            v_op(lambda e: e.tensor_tensor(gat[:, :], gat[:, :], gt5[:, :], Alu.mult),
                 drain=True)
            v_op(lambda e: e.reduce_sum(
                gt2[:, :], gat[:, :].rearrange("p (t e) -> p t e", e=8), axis=AxX), drain=True)
            v_op(lambda e: e.scalar_tensor_tensor(
                gt2[:, :], gt1[:, :], EPS_GATE, gt2[:, :], Alu.mult, Alu.add), drain=True)
            v_op(lambda e: e.reciprocal(gt1[:, :], gt2[:, :]), drain=True)
            gat_d = v_op(lambda e: e.tensor_tensor(
                gat16[:, :].rearrange("p (t e) -> p t e", e=8),
                gat[:, :].rearrange("p (t e) -> p t e", e=8),
                gt1[:, :].unsqueeze(2).broadcast_to([128, 32, 8]),
                Alu.mult), inc=True, drain=True)
            gh_d = v_op(lambda e: e.tensor_tensor(
                h_sb[:, :].rearrange("p (t e r) -> p t e r", e=8, r=16),
                h_sb[:, :].rearrange("p (t e r) -> p t e r", e=8, r=16),
                gat16[:, :].rearrange("p (t e) -> p t e", e=8)
                .unsqueeze(3).broadcast_to([128, 32, 8, 16]),
                Alu.mult), inc=True, drain=True)

            ghT_d = {}
            for tc in range(32):
                bi = 4 + (tc % 2)
                bank_wait(bi)
                if tc == 0:
                    W("tensor", "sDV", gh_d)
                pk = banks[bi]
                last = mm(pk[:, 0:128], h_sb[:, tc * 128:(tc + 1) * 128], id16[:, :],
                          True, True, inc="sPE")
                W("vector", "sPE", last)
                ghT_d[tc] = v_op(lambda e, tc=tc, pk=pk: e.tensor_copy(
                    ghT[:, tc * 128:(tc + 1) * 128], pk[:, 0:128]), inc=True)
                bank_guard[bi] = ("sDV", ghT_d[tc])
            P.emit("vector", lambda e: e.drain())

            asm_d = {}
            store_cnt = {}
            yss = ["sY0", "sY1"]
            for tc in range(32):
                bi0 = (tc % 2) * 2
                bank_wait(bi0)
                bank_wait(bi0 + 1)
                if tc == 0:
                    W("tensor", "sDV", ghT_d[31])
                pA, pB = banks[bi0], banks[bi0 + 1]
                for dc in range(8):
                    lhs = xdt[:, dc * 4096 + tc * 128: dc * 4096 + (tc + 1) * 128]
                    mm(pA[:, 0:512], lhs, wffn_sb[:, dc * 1024:dc * 1024 + 512],
                       dc == 0, False)
                    mm(pB[:, 0:512], lhs, wffn_sb[:, dc * 1024 + 512:(dc + 1) * 1024],
                       dc == 0, False)
                mm(pA[:, 0:512], ones1[0:1, :], bias_sb[0:1, 0:512], False, False)
                mm(pB[:, 0:512], ones1[0:1, :], bias_sb[0:1, 512:1024], False, False)
                mm(pA[:, 0:512], ghT[:, tc * 128:(tc + 1) * 128], bt2_sb[:, 0:512],
                   False, True)
                lastp = mm(pB[:, 0:512], ghT[:, tc * 128:(tc + 1) * 128],
                           bt2_sb[:, 512:1024], False, True, inc="sPE")
                sl = tc % 2
                W("gpsimd", "sA2A", 1)
                if tc >= 2:
                    W("gpsimd", "sDV", asm_d[tc - 2])
                g_dma(y_sb[:, sl * 1024:(sl + 1) * 1024].rearrange(
                    "p (j d) -> p j d", j=8),
                    d_a2aout[:, tc * 128:(tc + 1) * 128, :].rearrange("j t d -> t j d"),
                    yss[sl])
                W("vector", yss[sl], P.cnt(yss[sl]))
                v_op(lambda e, sl=sl, tc=tc: e.tensor_tensor(
                    ypl[:, sl * 1024:(sl + 1) * 1024],
                    y_sb[:, sl * 1024:(sl + 1) * 1024],
                    ysk[:, tc * 1024:(tc + 1) * 1024], Alu.add), drain=True)
                if tc >= 2:
                    W("vector", ("sST", "sST1")[tc % 2], store_cnt[tc - 2])
                W("vector", "sPE", lastp)
                v_op(lambda e, sl=sl, pA=pA: e.tensor_tensor(
                    osb[:, sl * 1024:sl * 1024 + 512], pA[:, 0:512],
                    ypl[:, sl * 1024:sl * 1024 + 512], Alu.add))
                ev = v_op(lambda e, sl=sl, pB=pB: e.tensor_tensor(
                    osb[:, sl * 1024 + 512:sl * 1024 + 1024], pB[:, 0:512],
                    ypl[:, sl * 1024 + 512:sl * 1024 + 1024], Alu.add), inc=True,
                    drain=True)
                asm_d[tc] = ev
                bank_guard[bi0] = ("sDV", ev)
                bank_guard[bi0 + 1] = ("sDV", ev)
                W("gpsimd", "sDV", ev)
                accsem = (sAC0, sAC1)[sl]
                P.emit("gpsimd", lambda e, sl=sl, tc=tc, accsem=accsem: e.dma_start(
                    out=osb[:, sl * 1024:(sl + 1) * 1024],
                    in_=d_utok[tc * 128:(tc + 1) * 128, :],
                    accum_op=Alu.add).then_inc(accsem, 16))
                acc = P.inc("sAC%d" % sl, 16)
                W("sync", "sAC%d" % sl, acc)
                store_cnt[tc] = s_dma(d_out[tc * 128:(tc + 1) * 128, :],
                                      osb[:, sl * 1024:(sl + 1) * 1024],
                                      ("sST", "sST1")[tc % 2])
            W("sync", "sST", P.cnt("sST"))
            W("sync", "sST1", P.cnt("sST1"))

        if trunc is not None:
            cut = P.marks[trunc]
            for e2 in P.ops:
                P.ops[e2] = P.ops[e2][:cut[e2]]

        @block.sync
        def _(e):
            for f in P.ops["sync"]:
                f(e)

        @block.gpsimd
        def _(e):
            for f in P.ops["gpsimd"]:
                f(e)

        @block.tensor
        def _(e):
            for f in P.ops["tensor"]:
                f(e)

        @block.vector
        def _(e):
            for f in P.ops["vector"]:
                f(e)

        @block.scalar
        def _(e):
            for f in P.ops["scalar"]:
                f(e)

    return nc


def kernel(L0_attn_u, L0_attn_k, L0_attn_skip, L0_ffn_W, L0_ffn_b, L0_ffn_R,
           L0_ffn_A, L0_ffn_B):
    u = np.ascontiguousarray(np.asarray(L0_attn_u, dtype=np.float32))
    kk = np.ascontiguousarray(np.asarray(L0_attn_k, dtype=np.float32))
    skip = np.asarray(L0_attn_skip, dtype=np.float32)
    Wf = np.asarray(L0_ffn_W, dtype=np.float32)
    bf = np.asarray(L0_ffn_b, dtype=np.float32)
    Rf = np.asarray(L0_ffn_R, dtype=np.float32)
    Af = np.asarray(L0_ffn_A, dtype=np.float32)
    Bf = np.asarray(L0_ffn_B, dtype=np.float32)

    fc, fpc, wt, st, ident, fs1 = _consts()
    w_do = np.ascontiguousarray(Wf.T)
    rt = np.ascontiguousarray(Rf.T)
    at = np.ascontiguousarray(Af.transpose(2, 0, 1).reshape(D, 128))
    bt2 = np.ascontiguousarray(Bf.transpose(0, 2, 1).reshape(128, D))
    bias_row = np.ascontiguousarray(bf.reshape(1, D))
    skiprep = np.ascontiguousarray(np.broadcast_to(skip.reshape(1, D), (128, D)))

    in_maps = []
    for c in range(NC):
        d0 = c * DL
        bc, hc = c // 2, c % 2
        ucv = u[:, :, d0:d0 + DL].reshape(B, 64, 128, DL)
        in_maps.append({
            "u_conv": np.ascontiguousarray(ucv),
            "u_tok": np.ascontiguousarray(u[bc, hc * NT:(hc + 1) * NT, :]),
            "k_conv": np.ascontiguousarray(
                kk[d0:d0 + DL, :].reshape(2, 64, 64, 128).transpose(0, 2, 3, 1)),
            "fc": fc, "fs1": fs1, "fpc": fpc, "wt": wt, "st": st, "identf": ident,
            "wffn": w_do, "biasv": bias_row, "skiprep": skiprep,
            "rt": rt, "at": at, "bt2": bt2,
        })

    nc = build_nc()
    with allow_ici_all_to_all():
        res = run_bass_kernel_spmd(nc, in_maps, core_ids=list(range(NC)))

    out = np.empty((B, L, D), np.float32)
    for c in range(NC):
        bc, hc = c // 2, c % 2
        out[bc, hc * NT:(hc + 1) * NT, :] = res.results[c]["out"]
    return out


if __name__ == "__main__":
    import importlib.util
    spec = importlib.util.spec_from_file_location("_ref", "/root/problem/reference.py")
    _r = importlib.util.module_from_spec(spec)
    spec.loader.exec_module(_r)
    inp = _r.setup_inputs()
    o = kernel(**{k: np.asarray(v) for k, v in inp.items()})
    ref = np.asarray(_r.reference(**inp))
    err = o - ref
    print("rel L2:", np.sqrt((err ** 2).mean() / (ref ** 2).mean()),
          "max abs:", np.abs(err).max())


# version 09 (final)
# speedup vs baseline: 1.3768x; 1.3768x over previous
import contextlib
import os
import numpy as np
import concourse.bass as bass
import concourse.mybir as mybir
from concourse.bass_utils import run_bass_kernel_spmd
from concourse.aot_env import allow_ici_all_to_all

F32 = mybir.dt.float32
F16 = mybir.dt.float16
Alu = mybir.AluOpType
Act = mybir.ActivationFunctionType
AxX = mybir.AxisListType.X

B, L, D = 4, 8192, 1024
NF = 2 * L
DL = 128
NT = 4096
NC = 8
EPS_LN = 1e-5
EPS_GATE = 1e-6
BIG = 1e9

_CONSTS = None


def _consts():
    global _CONSTS
    if _CONSTS is not None:
        return _CONSTS
    n = np.arange(128)
    th = 2.0 * np.pi * np.outer(n[:64], n) / 128.0
    fc = np.zeros((128, 3, 128), np.float32)
    fc[:64, 0], fc[:64, 1], fc[:64, 2] = np.cos(th), -np.sin(th), np.sin(th)
    fc[64:] = fc[:64]
    th = 2.0 * np.pi * np.outer(n, n) / 128.0
    fpc = np.stack([np.cos(th), np.sin(th), -np.sin(th), np.cos(th)], 1).astype(np.float32)
    k1g, n2g, k2g = n[:, None, None], n[None, :, None], n[None, None, :]
    th = 2.0 * np.pi * n2g * (k1g + 128 * k2g) / NF
    wt = np.stack([np.cos(th), -np.sin(th), np.sin(th)], 1).astype(np.float16)
    m1g, k1g2, m2g = n[:, None, None], n[None, :, None], np.arange(64)[None, None, :]
    th = 2.0 * np.pi * k1g2 * (m1g + 128 * m2g) / NF
    st = np.stack([np.cos(th), np.sin(th), -np.sin(th)], 1).astype(np.float16)
    ident = np.eye(128, dtype=np.float32)
    th2 = 2.0 * np.pi * np.outer(np.arange(64), np.arange(128)) / 128.0
    fs1 = np.zeros((128, 2, 128), np.float32)
    fs1[:64, 0], fs1[64:, 0] = np.cos(th2), np.sin(th2)
    fs1[:64, 1], fs1[64:, 1] = -np.sin(th2), np.cos(th2)
    _CONSTS = (fc, fpc, wt, st, ident, fs1)
    return _CONSTS


class Prog:
    def __init__(self):
        self.ops = {e: [] for e in ("sync", "gpsimd", "tensor", "vector", "scalar")}
        self.led = {}
        self.marks = {}

    def mark(self, name):
        self.marks[name] = {e: len(v) for e, v in self.ops.items()}

    def emit(self, eng, fn):
        self.ops[eng].append(fn)

    def inc(self, s, n=1):
        self.led[s] = self.led.get(s, 0) + n
        return self.led[s]

    def cnt(self, s):
        return self.led.get(s, 0)


def build_nc(trunc=None):
    nc = bass.Bass("TRN2", target_bir_lowering=False)

    d_uconv = nc.dram_tensor("u_conv", [B, 64, 128, DL], F32, kind="ExternalInput").ap()
    d_utok = nc.dram_tensor("u_tok", [NT, D], F32, kind="ExternalInput").ap()
    d_kconv = nc.dram_tensor("k_conv", [2, 64, 128, 64], F32, kind="ExternalInput").ap()
    d_fc = nc.dram_tensor("fc", [128, 3, 128], F32, kind="ExternalInput").ap()
    d_fs1 = nc.dram_tensor("fs1", [128, 2, 128], F32, kind="ExternalInput").ap()
    d_fpc = nc.dram_tensor("fpc", [128, 4, 128], F32, kind="ExternalInput").ap()
    d_wt = nc.dram_tensor("wt", [128, 3, 128, 128], F16, kind="ExternalInput").ap()
    d_st = nc.dram_tensor("st", [128, 3, 128, 64], F16, kind="ExternalInput").ap()
    d_idf = nc.dram_tensor("identf", [128, 128], F32, kind="ExternalInput").ap()
    d_wffn = nc.dram_tensor("wffn", [D, D], F32, kind="ExternalInput").ap()
    d_bias = nc.dram_tensor("biasv", [1, D], F32, kind="ExternalInput").ap()
    d_skip = nc.dram_tensor("skiprep", [128, D], F32, kind="ExternalInput").ap()
    d_rt = nc.dram_tensor("rt", [D, 8], F32, kind="ExternalInput").ap()
    d_at = nc.dram_tensor("at", [D, 128], F32, kind="ExternalInput").ap()
    d_bt2 = nc.dram_tensor("bt2", [128, D], F32, kind="ExternalInput").ap()
    d_out = nc.dram_tensor("out", [NT, D], F32, kind="ExternalOutput").ap()

    t_kdram = nc.dram_tensor("kdram", [128, 128, 256], F16)
    d_kdram = t_kdram.ap()
    t_agin = nc.dram_tensor("aginb", [1, 8192], F32)
    t_agout = nc.dram_tensor("agoutb", [8, 8192], F32)
    t_a2ain = nc.dram_tensor("a2ainb", [NC, NT, DL], F16)
    t_a2aout = nc.dram_tensor("a2aoutb", [NC, NT, DL], F16)
    d_agin, d_agout = t_agin.ap(), t_agout.ap()
    d_a2ain, d_a2aout = t_a2ain.ap(), t_a2aout.ap()

    P = Prog()

    with contextlib.ExitStack() as _st:
        EC = _st.enter_context
        fc_sb = EC(nc.sbuf_tensor("fc_sb", [128, 384], F16))
        fs1_sb = EC(nc.sbuf_tensor("fs1_sb", [128, 256], F16))
        fpc_sb = EC(nc.sbuf_tensor("fpc_sb", [128, 512], F16))
        id16 = EC(nc.sbuf_tensor("id16", [128, 128], F16))
        id32 = EC(nc.sbuf_tensor("id32", [128, 128], F32))
        ones1 = EC(nc.sbuf_tensor("ones1", [1, 128], F16))
        stats_sb = EC(nc.sbuf_tensor("stats", [128, 64], F32))
        statsall = EC(nc.sbuf_tensor("statsall", [128, 512], F32))
        ic0 = EC(nc.sbuf_tensor("ic0", [128, 128], F32))
        ic1 = EC(nc.sbuf_tensor("ic1", [128, 128], F32))
        gc0 = EC(nc.sbuf_tensor("gc0", [128, 128], F32))
        gc1 = EC(nc.sbuf_tensor("gc1", [128, 128], F32))
        wbuf = EC(nc.sbuf_tensor("wbuf", [128, 3072], F16))
        gt1 = EC(nc.sbuf_tensor("gt1", [128, 32], F32))
        gt2 = EC(nc.sbuf_tensor("gt2", [128, 32], F32))
        gt3 = EC(nc.sbuf_tensor("gt3", [128, 256], F32))
        pb0 = EC(nc.psum_tensor("pb0", [128, 512], F32))
        pb1 = EC(nc.psum_tensor("pb1", [128, 512], F32))
        pb2 = EC(nc.psum_tensor("pb2", [128, 512], F32))
        pb3 = EC(nc.psum_tensor("pb3", [128, 512], F32))
        pb4 = EC(nc.psum_tensor("pb4", [128, 512], F32))
        pb5 = EC(nc.psum_tensor("pb5", [128, 512], F32))
        pb6 = EC(nc.psum_tensor("pb6", [128, 512], F32))
        pb7 = EC(nc.psum_tensor("pb7", [128, 512], F32))
        sL = EC(nc.semaphore("sL"))
        sG = EC(nc.semaphore("sG"))
        sPE = EC(nc.semaphore("sPE"))
        sDV = EC(nc.semaphore("sDV"))
        sSC = EC(nc.semaphore("sSC"))
        sAG = EC(nc.semaphore("sAG"))
        sA2A = EC(nc.semaphore("sA2A"))
        sST = EC(nc.semaphore("sST"))
        sW0 = EC(nc.semaphore("sW0"))
        sW1 = EC(nc.semaphore("sW1"))
        sU0 = EC(nc.semaphore("sU0"))
        sU1 = EC(nc.semaphore("sU1"))
        sK0 = EC(nc.semaphore("sK0"))
        sK1 = EC(nc.semaphore("sK1"))
        sT0 = EC(nc.semaphore("sT0"))
        sT1 = EC(nc.semaphore("sT1"))
        sY0 = EC(nc.semaphore("sY0"))
        sY1 = EC(nc.semaphore("sY1"))
        sGC = EC(nc.semaphore("sGC"))
        sKD0 = EC(nc.semaphore("sKD0"))
        sKD1 = EC(nc.semaphore("sKD1"))
        sYS0 = EC(nc.semaphore("sYS0"))
        sYS1 = EC(nc.semaphore("sYS1"))
        sAC0 = EC(nc.semaphore("sAC0"))
        sAC1 = EC(nc.semaphore("sAC1"))
        sST1 = EC(nc.semaphore("sST1"))
        sW2 = EC(nc.semaphore("sW2"))
        block = EC(nc.Block())

        banks = [pb0, pb1, pb2, pb3, pb4, pb5, pb6, pb7]
        bank_guard = [None] * 8
        sems = dict(sL=sL, sG=sG, sPE=sPE, sDV=sDV, sSC=sSC, sAG=sAG,
                    sA2A=sA2A, sST=sST, sW0=sW0, sW1=sW1, sU0=sU0, sU1=sU1,
                    sK0=sK0, sK1=sK1, sT0=sT0, sT1=sT1, sY0=sY0, sY1=sY1,
                    sGC=sGC, sKD0=sKD0, sKD1=sKD1, sYS0=sYS0, sYS1=sYS1,
                    sAC0=sAC0, sAC1=sAC1, sST1=sST1, sW2=sW2)

        def W(eng, sname, val):
            if val <= 0:
                return
            s = sems[sname]
            P.emit(eng, lambda e, s=s, v=val: e.wait_ge(s, v))

        def g_dma(out_ap, in_ap, sname="sG"):
            s = sems[sname]
            P.emit("gpsimd", lambda e, o=out_ap, i=in_ap, s=s:
                   e.dma_start(out=o, in_=i).then_inc(s, 16))
            return P.inc(sname, 16)

        def s_dma(out_ap, in_ap, sname="sL"):
            s = sems[sname]
            P.emit("sync", lambda e, o=out_ap, i=in_ap, s=s:
                   e.dma_start(out=o, in_=i).then_inc(s, 16))
            return P.inc(sname, 16)

        def mm(out_ap, lhsT_ap, rhs_ap, start, stop, inc=None, tp=None, sgc=False):
            def f(e, o=out_ap, l=lhsT_ap, r=rhs_ap, st=start, sp=stop, inc=inc, tp=tp, sgc=sgc):
                i = e.matmul(o, l, r, start=st, stop=sp, tile_position=tp,
                             skip_group_check=sgc)
                if inc is not None:
                    i.then_inc(sems[inc], 1)
            P.emit("tensor", f)
            if inc is not None:
                return P.inc(inc)

        def v_op(fn, inc=False, drain=False):
            if inc:
                P.emit("vector", lambda e, f=fn: f(e).then_inc(sDV, 1))
                r = P.inc("sDV")
            else:
                P.emit("vector", fn)
                r = None
            if drain:
                P.emit("vector", lambda e: e.drain())
            return r

        def sc_op(fn, inc=False):
            if inc:
                P.emit("scalar", lambda e, f=fn: f(e).then_inc(sSC, 1))
                return P.inc("sSC")
            P.emit("scalar", fn)

        def bank_wait(bi):
            g = bank_guard[bi]
            if g is not None:
                W("tensor", g[0], g[1])
                bank_guard[bi] = None

        with contextlib.ExitStack() as _cs:
            ECC = _cs.enter_context
            rga = ECC(nc.sbuf_tensor("rga", [128, 32768], F16))
            rgb = ECC(nc.sbuf_tensor("rgb", [128, 32768], F16))
            ust = ECC(nc.sbuf_tensor("ust", [128, 2048], F32))
            tst = ECC(nc.sbuf_tensor("tst", [128, 1024], F32))
            kstg = ECC(nc.sbuf_tensor("kstg", [128, 2048], F16))
            kpw = ECC(nc.sbuf_tensor("kpw", [128, 2048], F16))
            ptmp = ECC(nc.sbuf_tensor("ptmp", [128, 2048], F16))
            ystg = ECC(nc.sbuf_tensor("ystg", [128, 4096], F16))

            _es = contextlib.ExitStack()
            k_sb = _es.enter_context(nc.sbuf_tensor("k_sb", [128, 8192], F16, side="right"))
            utst = _es.enter_context(nc.sbuf_tensor("utst", [128, 2048], F32, side="right"))
            sqtmp = _es.enter_context(nc.sbuf_tensor("sqtmp", [128, 1024], F32, side="right"))

            kc = d_kconv.rearrange("dh n1 n2 dl -> (dh n1) n2 dl")
            g_dma(k_sb[0:64, :].rearrange("p (n2 dl) -> p n2 dl", dl=64), kc[0:64], "sGC")
            g_dma(k_sb[64:128, :].rearrange("p (n2 dl) -> p n2 dl", dl=64), kc[64:128], "sGC")
            g_dma(fc_sb[:, :].rearrange("p (a b) -> p a b", a=3), d_fc, "sGC")
            g_dma(fs1_sb[:, :].rearrange("p (a b) -> p a b", a=2), d_fs1, "sGC")
            g_dma(fpc_sb[:, :].rearrange("p (a b) -> p a b", a=4), d_fpc, "sGC")
            g_dma(id16[:, :], d_idf, "sGC")
            ld_k = P.cnt("sGC")
            s_dma(id32[:, :], d_idf)
            ld_id32 = P.cnt("sL")
            P.emit("vector", lambda e: e.memset(ones1[:, :], 1.0))
            P.emit("vector", lambda e: e.drain())

            P.mark('p0')
            tok_ss = ["sT0", "sT1"]
            red_done = {}
            for tc in range(32):
                sl = tc % 2
                if tc >= 2:
                    W("sync", "sDV", red_done[tc - 2])
                s_dma(utst[:, sl * 1024:(sl + 1) * 1024],
                      d_utok[tc * 128:(tc + 1) * 128, :], tok_ss[sl])
                W("scalar", tok_ss[sl], P.cnt(tok_ss[sl]))
                if tc >= 1:
                    W("scalar", "sDV", red_done[tc - 1])
                sq = sc_op(lambda e, sl=sl: e.activation(
                    out=sqtmp[:, :], in_=utst[:, sl * 1024:(sl + 1) * 1024],
                    func=Act.Square), inc=True)
                W("vector", tok_ss[sl], P.cnt(tok_ss[sl]))
                v_op(lambda e, sl=sl, tc=tc: e.reduce_sum(
                    stats_sb[:, tc:tc + 1], utst[:, sl * 1024:(sl + 1) * 1024], axis=AxX))
                W("vector", "sSC", sq)
                red_done[tc] = v_op(lambda e, tc=tc: e.reduce_sum(
                    stats_sb[:, 32 + tc:32 + tc + 1], sqtmp[:, :], axis=AxX), inc=True)
            P.emit("vector", lambda e: e.drain())
            v_op(lambda e: e.tensor_scalar(gt1[:, :], stats_sb[:, 0:32], 1.0 / D, None, Alu.mult), drain=True)
            v_op(lambda e: e.tensor_scalar(gt2[:, :], stats_sb[:, 32:64], 1.0 / D, None, Alu.mult), drain=True)
            v_op(lambda e: e.tensor_tensor(gt3[:, 0:32], gt1[:, :], gt1[:, :], Alu.mult), drain=True)
            v_op(lambda e: e.tensor_tensor(gt2[:, :], gt2[:, :], gt3[:, 0:32], Alu.subtract), drain=True)
            v_op(lambda e: e.tensor_scalar(gt2[:, :], gt2[:, :], EPS_LN, None, Alu.add), drain=True)
            rec = v_op(lambda e: e.reciprocal(gt3[:, 0:32], gt2[:, :]), inc=True, drain=True)
            W("scalar", "sDV", rec)
            inv_d = sc_op(lambda e: e.activation(out=stats_sb[:, 0:32], in_=gt3[:, 0:32],
                                                 func=Act.Sqrt), inc=True)
            W("vector", "sSC", inv_d)
            v_op(lambda e: e.tensor_tensor(gt2[:, :], gt1[:, :], stats_sb[:, 0:32], Alu.mult), drain=True)
            stats_fin = v_op(lambda e: e.tensor_scalar(
                stats_sb[:, 32:64], gt2[:, :], -1.0, None, Alu.mult), inc=True, drain=True)

            W("gpsimd", "sDV", stats_fin)
            g_dma(d_agin.rearrange("o (p c) -> (o p) c", p=128), stats_sb[:, :])
            W("gpsimd", "sG", P.cnt("sG"))
            P.emit("gpsimd", lambda e: e.collective_compute(
                "AllGather", Alu.bypass, replica_groups=[list(range(NC))],
                ins=[t_agin.ap().opt()], outs=[t_agout.ap().opt()]).then_inc(sAG, 1))
            P.inc("sAG")
            W("gpsimd", "sAG", 1)
            g_dma(statsall[:, :].rearrange("p (c w) -> p c w", c=8),
                  d_agout.rearrange("c (p w) -> p c w", p=128))
            ld_statsall = P.cnt("sG")

            P.mark('stats')
            W("tensor", "sG", ld_statsall)
            W("tensor", "sL", ld_id32)
            for p2 in range(2):
                for ty in range(2):
                    bi = 2 * p2 + ty
                    bank_wait(bi)
                    pk = banks[bi]
                    last = None
                    for b2 in range(2):
                        for h in range(2):
                            q = 2 * b2 + h
                            co = (2 * (2 * p2 + b2) + h) * 64 + ty * 32
                            last = mm(pk[32 * q:32 * q + 32, 0:128],
                                      statsall[:, co:co + 32], id32[:, :],
                                      True, True,
                                      inc="sPE" if q == 3 else None,
                                      tp=(0, 32 * q))
                    dst = (ic0, ic1)[p2] if ty == 0 else (gc0, gc1)[p2]
                    W("vector", "sPE", last)
                    v_op(lambda e, dst=dst, pk=pk: e.tensor_copy(dst[:, :], pk[:, 0:128]),
                         inc=True)
                    bank_guard[bi] = ("sDV", P.cnt("sDV"))
            cstats_done = P.cnt("sDV")
            P.emit("vector", lambda e: e.drain())

            P.mark('statsT')
            W("tensor", "sGC", ld_k)
            ks1_last = None
            for d in range(DL):
                dh, dl = d // 64, d % 64
                bi = (d // 2) % 8
                half = d % 2
                if half == 0:
                    bank_wait(bi)
                lhs = k_sb[dh * 64:(dh + 1) * 64, dl:8192:64]
                tp = (64, 0) if dh else None
                pk = banks[bi]
                last = mm(pk[:, half * 256:half * 256 + 256], lhs,
                          fc_sb[dh * 64:(dh + 1) * 64, 0:256], True, True,
                          inc="sPE" if half == 1 else None, tp=tp)
                if half == 1:
                    ks1_last = last
                    eng = "vector" if (d // 2) % 2 == 0 else "scalar"
                    W(eng, "sPE", last)
                    dst = rgb[:, (d - 1) * 256:(d + 1) * 256]
                    if eng == "vector":
                        v_op(lambda e, dst=dst, pk=pk: e.tensor_copy(dst, pk[:, :]), inc=True)
                        bank_guard[bi] = ("sDV", P.cnt("sDV"))
                    else:
                        sc_op(lambda e, dst=dst, pk=pk: e.copy(dst, pk[:, :]), inc=True)
                        bank_guard[bi] = ("sSC", P.cnt("sSC"))

            P.mark('ks1')
            ak_v, ak_s = P.cnt("sDV"), P.cnt("sSC")
            wss = ["sW0", "sW1"]
            pe_blk = {}
            kdram_cnt = {}
            for kb in range(32):
                sl = kb % 2
                if kb >= 2:
                    W("sync", "sPE", pe_blk[kb - 2])
                    W("scalar", "sKD%d" % (kb % 2), kdram_cnt[kb - 2])
                s_dma(wbuf[:, sl * 1536:(sl + 1) * 1536].rearrange(
                    "p (a b c) -> p a b c", a=4, b=3),
                    d_wt[kb * 4:(kb + 1) * 4].rearrange("a b p c -> p a b c"), wss[sl])
                W("tensor", wss[sl], P.cnt(wss[sl]))
                if kb == 0:
                    W("tensor", "sDV", ak_v)
                    W("tensor", "sSC", ak_s)
                for j in range(4):
                    k1 = kb * 4 + j
                    base = sl * 1536 + j * 384
                    wr = wbuf[:, base:base + 128]
                    wi = wbuf[:, base + 128:base + 256]
                    wn = wbuf[:, base + 256:base + 384]
                    bi = 4 + (k1 % 2)
                    bank_wait(bi)
                    pk = banks[bi]
                    akr = rgb[:, k1:32768:256]
                    aki = rgb[:, 128 + k1:32768:256]
                    mm(pk[:, 0:128], wr, akr, True, False)
                    mm(pk[:, 0:128], wn, aki, False, True)
                    mm(pk[:, 128:256], wi, akr, True, False)
                    mk = mm(pk[:, 128:256], wr, aki, False, True, inc="sPE")
                    pe_blk[kb] = mk
                    W("scalar", "sPE", mk)
                    sc_op(lambda e, pk=pk, sl=sl, j=j: e.activation(
                        out=kstg[:, sl * 1024 + j * 256: sl * 1024 + (j + 1) * 256],
                        in_=pk[:, 0:256], func=Act.Copy, scale=1.0 / NF), inc=True)
                    bank_guard[bi] = ("sSC", P.cnt("sSC"))
                W("gpsimd", "sSC", P.cnt("sSC"))
                kdram_cnt[kb] = g_dma(
                    d_kdram[kb * 4:(kb + 1) * 4].rearrange("a p c -> p a c"),
                    kstg[:, sl * 1024:(sl + 1) * 1024].rearrange("p (a c) -> p a c", a=4),
                    "sKD%d" % (kb % 2))
            kdram_done = (P.cnt("sKD0"), P.cnt("sKD1"))

            P.mark('ks2')
            _dbg0 = os.environ.get("S1X", "")
            if "M" in _dbg0:
                for _q in range(4):
                    mm(banks[0][:, _q * 128:(_q + 1) * 128], fpc_sb[:, 0:128],
                       fpc_sb[:, 128:256], True, True)
                P.mark('premm')
            _es.close()
            rxn = ECC(nc.sbuf_tensor("rxn", [128, 16384], F16))
            if "N" in _dbg0:
                for _q in range(4):
                    mm(banks[0][:, _q * 128:(_q + 1) * 128], fpc_sb[:, 0:128],
                       fpc_sb[:, 128:256], True, True)
                P.mark('postmm')

            uss = ["sU0", "sU1"]
            kss = ["sK0", "sK1"]
            xn_chunk = {}
            s1_last = {}
            is2_blk = {}
            ystg_dma = {}
            pw_chunk = {}
            z_last = None
            for pr in range(2):
                for ck in range(16):
                    if pr == 0:
                        P.mark('p1ck%d' % ck)
                    sl = ck % 2
                    if pr > 0 or ck >= 2:
                        W("sync", "sDV", xn_chunk[(pr, ck - 2)] if ck >= 2
                          else xn_chunk[(pr - 1, 14 + ck)])
                    if pr > 0 and ck < 2:
                        W("sync", "sPE", s1_last[pr - 1])
                    if "R" in os.environ.get("S1X", ""):
                        src = d_utok[0:128, 0:1024]
                        s_dma(ust[:, sl * 1024:(sl + 1) * 1024], src, uss[sl])
                    else:
                        src = d_uconv[2 * pr:2 * pr + 2, :, ck * 8:(ck + 1) * 8, :].rearrange(
                            "b n1 n2 d -> (b n1) n2 d")
                        s_dma(ust[:, sl * 1024:(sl + 1) * 1024].rearrange(
                            "p (n2 d) -> p n2 d", d=128), src, uss[sl])
                    W("vector", uss[sl], P.cnt(uss[sl]))
                    if pr == 0 and ck < 2:
                        W("vector", "sDV", cstats_done)
                        W("vector", "sPE", pe_blk[1] if ck == 0 else pe_blk[1])
                    if pr > 0 and ck < 2:
                        W("vector", "sPE", s1_last[pr - 1])
                    iv, gv = (ic0, ic1)[pr], (gc0, gc1)[pr]
                    _dbgP = os.environ.get("S1X", "")
                    if "P" in _dbgP:
                        xn_chunk[(pr, ck)] = v_op(lambda e, sl=sl, ck=ck: e.tensor_copy(
                            rxn[:, ck * 1024:(ck + 1) * 1024],
                            ust[:, sl * 1024:(sl + 1) * 1024]), inc=True, drain=True)
                    else:
                        v_op(lambda e, sl=sl, iv=iv, ck=ck: e.tensor_tensor(
                            tst[:, :].rearrange("p (n d) -> p n d", d=128),
                            ust[:, sl * 1024:(sl + 1) * 1024].rearrange("p (n d) -> p n d", d=128),
                            iv[:, ck * 8:(ck + 1) * 8].unsqueeze(2).broadcast_to([128, 8, 128]),
                            Alu.mult), drain=True)
                        xn_chunk[(pr, ck)] = v_op(lambda e, gv=gv, ck=ck: e.tensor_tensor(
                            rxn[:, ck * 1024:(ck + 1) * 1024].rearrange("p (n d) -> p n d", d=128),
                            tst[:, :].rearrange("p (n d) -> p n d", d=128),
                            gv[:, ck * 8:(ck + 1) * 8].unsqueeze(2).broadcast_to([128, 8, 128]),
                            Alu.add), inc=True, drain=True)
                xn_all = xn_chunk[(pr, 15)]
                P.mark('xn%d' % pr)
                if pr == 0 and "T" in os.environ.get("S1X", ""):
                    for _q in range(4):
                        mm(banks[0][:, _q * 128:(_q + 1) * 128], fpc_sb[:, 0:128],
                           fpc_sb[:, 128:256], True, True)
                    P.mark('xnmm')

                if "C" not in os.environ.get("S1X", ""):
                    W("tensor", "sDV", xn_all)
                for dl in range(DL):
                    bi = (dl // 2) % 8
                    half = dl % 2
                    if half == 0:
                        bank_wait(bi)
                    zz = rxn[:, dl:16384:128]
                    pk = banks[bi]
                    last = mm(pk[:, half * 256:half * 256 + 256], zz,
                              fs1_sb[:, 0:256], True, True, inc="sPE")
                    if dl in (1, 15, 63):
                        P.mark('s1_%d_dl%d' % (pr, dl))
                    if half == 1:
                        s1_last[pr] = last
                        eng = "vector" if (dl // 2) % 2 == 0 else "scalar"
                        W(eng, "sPE", last)
                        dst = rga[:, (dl - 1) * 256:(dl + 1) * 256]
                        if eng == "vector":
                            v_op(lambda e, dst=dst, pk=pk: e.tensor_copy(dst, pk[:, :]), inc=True)
                            bank_guard[bi] = ("sDV", P.cnt("sDV"))
                        else:
                            sc_op(lambda e, dst=dst, pk=pk: e.copy(dst, pk[:, :]), inc=True)
                            bank_guard[bi] = ("sSC", P.cnt("sSC"))
                at_v, at_s = P.cnt("sDV"), P.cnt("sSC")
                P.mark('s1_%d' % pr)

                pe_blk2 = {}
                wslot3 = [wbuf[:, 0:1536], wbuf[:, 1536:3072], kstg[:, 0:1536]]
                wss3 = ["sW0", "sW1", "sW2"]
                for kb in range(32):
                    sl = kb % 3
                    W("sync", "sPE", pe_blk2[kb - 3] if kb >= 3 else
                      (pe_blk[31] if pr == 0 else is2_blk[(0, 15)]))
                    if pr == 0 and kb == 2:
                        W("sync", "sKD0", kdram_done[0])
                        W("sync", "sKD1", kdram_done[1])
                    s_dma(wslot3[sl].rearrange(
                        "p (a b c) -> p a b c", a=4, b=3),
                        d_wt[kb * 4:(kb + 1) * 4].rearrange("a b p c -> p a b c"), wss3[sl])
                    W("tensor", wss3[sl], P.cnt(wss3[sl]))
                    if kb == 0:
                        W("tensor", "sDV", at_v)
                        W("tensor", "sSC", at_s)
                    for j in range(4):
                        k1 = kb * 4 + j
                        base = j * 384
                        wr = wslot3[sl][:, base:base + 128]
                        wi = wslot3[sl][:, base + 128:base + 256]
                        wn = wslot3[sl][:, base + 256:base + 384]
                        bi = (k1 // 2) % 2
                        qq = k1 % 2
                        if qq == 0:
                            bank_wait(bi)
                        pk = banks[bi]
                        ar = rga[:, k1:32768:256]
                        ai = rga[:, 128 + k1:32768:256]
                        o_re = pk[:, qq * 256:qq * 256 + 128]
                        o_im = pk[:, qq * 256 + 128:qq * 256 + 256]
                        mm(o_re, wr, ar, True, False)
                        mm(o_re, wn, ai, False, True)
                        mm(o_im, wi, ar, True, False)
                        mz = mm(o_im, wr, ai, False, True, inc="sPE")
                        pe_blk2[kb] = mz
                        z_last = mz
                        if qq == 1:
                            eng = "vector" if (k1 // 2) % 2 == 0 else "scalar"
                            W(eng, "sPE", mz)
                            dst = rgb[:, (k1 - 1) * 256:(k1 + 1) * 256]
                            if eng == "vector":
                                v_op(lambda e, dst=dst, pk=pk: e.tensor_copy(dst, pk[:, :]), inc=True)
                                bank_guard[bi] = ("sDV", P.cnt("sDV"))
                            else:
                                sc_op(lambda e, dst=dst, pk=pk: e.copy(dst, pk[:, :]), inc=True)
                                bank_guard[bi] = ("sSC", P.cnt("sSC"))
                z_v, z_s = P.cnt("sDV"), P.cnt("sSC")
                P.mark('s2_%d' % pr)

                if pr == 0:
                    W("sync", "sKD0", kdram_done[0])
                    W("sync", "sKD1", kdram_done[1])
                for kc2 in range(32):
                    sl = kc2 % 2
                    if pr > 0 or kc2 >= 2:
                        W("sync", "sDV", pw_chunk[(pr, kc2 - 2)] if kc2 >= 2
                          else pw_chunk[(pr - 1, 30 + kc2)])
                    s_dma(kpw[:, sl * 1024:(sl + 1) * 1024].rearrange(
                        "p (a c) -> p a c", a=4),
                        d_kdram[kc2 * 4:(kc2 + 1) * 4].rearrange("a p c -> p a c"), kss[sl])
                    W("vector", kss[sl], P.cnt(kss[sl]))
                    if kc2 < 2:
                        W("vector", "sDV", z_v)
                        W("vector", "sSC", z_s)
                    zre = rgb[:, kc2 * 1024:(kc2 + 1) * 1024].rearrange(
                        "p (k a) -> p k a", a=256)[:, :, 0:128]
                    zim = rgb[:, kc2 * 1024:(kc2 + 1) * 1024].rearrange(
                        "p (k a) -> p k a", a=256)[:, :, 128:256]
                    kre = kpw[:, sl * 1024:(sl + 1) * 1024].rearrange(
                        "p (k a) -> p k a", a=256)[:, :, 0:128]
                    kim = kpw[:, sl * 1024:(sl + 1) * 1024].rearrange(
                        "p (k a) -> p k a", a=256)[:, :, 128:256]
                    t1 = ptmp[:, 0:512].rearrange("p (k a) -> p k a", a=128)
                    t2 = ptmp[:, 512:1024].rearrange("p (k a) -> p k a", a=128)
                    t3 = ptmp[:, 1024:1536].rearrange("p (k a) -> p k a", a=128)
                    t4 = ptmp[:, 1536:2048].rearrange("p (k a) -> p k a", a=128)
                    wre = rga[:, kc2 * 1024:(kc2 + 1) * 1024].rearrange(
                        "p (k a) -> p k a", a=256)[:, :, 0:128]
                    wim = rga[:, kc2 * 1024:(kc2 + 1) * 1024].rearrange(
                        "p (k a) -> p k a", a=256)[:, :, 128:256]
                    v_op(lambda e, a=t1, b=zre, c=kre: e.tensor_tensor(a, b, c, Alu.mult))
                    v_op(lambda e, a=t2, b=zim, c=kim: e.tensor_tensor(a, b, c, Alu.mult))
                    v_op(lambda e, a=t3, b=zre, c=kim: e.tensor_tensor(a, b, c, Alu.mult))
                    v_op(lambda e, a=t4, b=zim, c=kre: e.tensor_tensor(a, b, c, Alu.mult),
                         drain=True)
                    v_op(lambda e, a=wre, b=t1, c=t2: e.tensor_tensor(a, b, c, Alu.subtract))
                    pw_chunk[(pr, kc2)] = v_op(lambda e, a=wim, b=t3, c=t4: e.tensor_tensor(
                        a, b, c, Alu.add), inc=True, drain=True)
                pw_all = pw_chunk[(pr, 31)]
                P.mark('pw_%d' % pr)

                W("tensor", "sDV", pw_all)
                for dl in range(DL):
                    bi = (dl // 2) % 8
                    half = dl % 2
                    if half == 0:
                        bank_wait(bi)
                    wre = rga[:, dl:32768:256]
                    wim = rga[:, 128 + dl:32768:256]
                    pk = banks[bi]
                    ww = pk[:, half * 256:half * 256 + 256]
                    mm(ww, wre, fpc_sb[:, 0:256], True, False)
                    last = mm(ww, wim, fpc_sb[:, 256:512], False, True, inc="sPE")
                    if half == 1:
                        eng = "vector" if (dl // 2) % 2 == 0 else "scalar"
                        W(eng, "sPE", last)
                        dst = rgb[:, (dl - 1) * 256:(dl + 1) * 256]
                        if eng == "vector":
                            v_op(lambda e, dst=dst, pk=pk: e.tensor_copy(dst, pk[:, :]), inc=True)
                            bank_guard[bi] = ("sDV", P.cnt("sDV"))
                        else:
                            sc_op(lambda e, dst=dst, pk=pk: e.copy(dst, pk[:, :]), inc=True)
                            bank_guard[bi] = ("sSC", P.cnt("sSC"))
                bt_v, bt_s = P.cnt("sDV"), P.cnt("sSC")
                P.mark('is1_%d' % pr)

                for mb in range(16):
                    sl = mb % 2
                    W("sync", "sPE", is2_blk[(pr, mb - 2)] if mb >= 2 else z_last)
                    s_dma(wbuf[:, sl * 1536:(sl + 1) * 1536].rearrange(
                        "p (a b c) -> p a b c", a=8, b=3),
                        d_st[mb * 8:(mb + 1) * 8].rearrange("a b p c -> p a b c"), wss[sl])
                    W("tensor", wss[sl], P.cnt(wss[sl]))
                    if mb == 0:
                        W("tensor", "sDV", bt_v)
                        W("tensor", "sSC", bt_s)
                    if pr > 0 or mb >= 2:
                        pv = ystg_dma[(pr, mb - 2)] if mb >= 2 else ystg_dma[(pr - 1, 14 + mb)]
                        W("vector", "sYS%d" % (mb % 2), pv)
                    for j in range(8):
                        m1 = mb * 8 + j
                        bi = (m1 // 2) % 4
                        qq = m1 % 2
                        if qq == 0:
                            bank_wait(bi)
                        base = sl * 1536 + j * 192
                        sr = wbuf[:, base:base + 64]
                        si = wbuf[:, base + 64:base + 128]
                        sn = wbuf[:, base + 128:base + 192]
                        btr = rgb[:, m1:32768:256]
                        bti = rgb[:, 128 + m1:32768:256]
                        pk = banks[bi]
                        o_re = pk[0:64, qq * 256:qq * 256 + 128]
                        o_im = pk[0:64, qq * 256 + 128:qq * 256 + 256]
                        mm(o_re, sr, btr, True, False)
                        mm(o_re, sn, bti, False, True)
                        mm(o_im, si, btr, True, False)
                        lastm = mm(o_im, sr, bti, False, True, inc="sPE")
                        is2_blk[(pr, mb)] = lastm
                        if qq == 1:
                            W("vector", "sPE", lastm)
                            ybase = sl * 2048 + (j - 1) * 256
                            v_op(lambda e, pk=pk, ybase=ybase: e.tensor_copy(
                                ystg[0:64, ybase:ybase + 512], pk[0:64, 0:512]), inc=True)
                            bank_guard[bi] = ("sDV", P.cnt("sDV"))
                    W("gpsimd", "sDV", P.cnt("sDV"))
                    for b2 in range(2):
                        for h in range(2):
                            jd = (2 * pr + b2) * 2 + h
                            src = ystg[h * 32:h * 32 + 32,
                                       sl * 2048:(sl + 1) * 2048].rearrange(
                                       "p (a c b) -> p a c b", a=8, c=2)[
                                       :, :, b2:b2 + 1, :].squeeze(2)
                            dst = d_a2ain[jd].rearrange(
                                "(m2 m1) d -> m2 m1 d", m1=128)[:, mb * 8:(mb + 1) * 8, :]
                            g_dma(dst, src, "sYS%d" % (mb % 2))
                    ystg_dma[(pr, mb)] = P.cnt("sYS%d" % (mb % 2))
                P.mark('is2_%d' % pr)

            P.mark('pairs')
            W("gpsimd", "sYS0", P.cnt("sYS0"))
            W("gpsimd", "sYS1", P.cnt("sYS1"))

            def _do_a2a(e):
                with allow_ici_all_to_all():
                    e.collective_compute(
                        "AllToAll", Alu.bypass, replica_groups=[list(range(NC))],
                        ins=[t_a2ain.ap().opt()],
                        outs=[t_a2aout.ap().opt()]).then_inc(sA2A, 1)
            P.emit("gpsimd", _do_a2a)
            P.inc("sA2A")
            conv_pe_done = P.cnt("sPE")

        P.mark('a2a')
        with contextlib.ExitStack() as _fs:
            ECF = _fs.enter_context
            xdt = ECF(nc.sbuf_tensor("xdt", [128, 32768], F16))
            ysk = ECF(nc.sbuf_tensor("ysk", [128, 32768], F16))
            wffn_sb = ECF(nc.sbuf_tensor("wffn_sb", [128, 8192], F16))
            at_sb = ECF(nc.sbuf_tensor("at_sb", [128, 1024], F16))
            rt_sb = ECF(nc.sbuf_tensor("rt_sb", [128, 64], F16))
            bt2_sb = ECF(nc.sbuf_tensor("bt2_sb", [128, 1024], F16))
            bias_sb = ECF(nc.sbuf_tensor("bias_sb", [1, 1024], F16))
            skrep = ECF(nc.sbuf_tensor("skrep", [128, 1024], F16))
            xtn = ECF(nc.sbuf_tensor("xtn", [128, 2048], F16))
            h_sb = ECF(nc.sbuf_tensor("h_sb", [128, 4096], F16))
            ghT = ECF(nc.sbuf_tensor("ghT", [128, 4096], F16))
            lg_sb = ECF(nc.sbuf_tensor("lg_sb", [128, 256], F32))
            gat = ECF(nc.sbuf_tensor("gat", [128, 256], F32))
            gat16 = ECF(nc.sbuf_tensor("gat16", [128, 256], F16))
            y_sb = ECF(nc.sbuf_tensor("y_sb", [128, 2048], F16))
            ypl = ECF(nc.sbuf_tensor("ypl", [128, 2048], F16))
            osb = ECF(nc.sbuf_tensor("osb", [128, 2048], F32))
            ut2 = ypl[:, 0:2048].bitcast(F32)
            gt4 = osb[:, 0:256]
            gt5 = lg_sb

            g_dma(wffn_sb[:, :].rearrange("p (c o) -> p c o", c=8),
                  d_wffn.rearrange("(c p) o -> p c o", c=8))
            g_dma(at_sb[:, :].rearrange("p (c o) -> p c o", c=8),
                  d_at.rearrange("(c p) o -> p c o", c=8))
            g_dma(rt_sb[:, :].rearrange("p (c o) -> p c o", c=8),
                  d_rt.rearrange("(c p) o -> p c o", c=8))
            g_dma(bt2_sb[:, :], d_bt2)
            g_dma(bias_sb[:, :], d_bias)
            g_dma(skrep[:, :], d_skip)
            ld_ffn = P.cnt("sG")

            xdt_done = {}
            nrm_done = {}
            for tc in range(32):
                sl = tc % 2
                if tc == 0:
                    W("sync", "sPE", conv_pe_done)
                if tc >= 1:
                    W("sync", "sDV", nrm_done[tc - 1])
                s_dma(ut2,
                      d_utok[tc * 128:(tc + 1) * 128, :], ["sT0", "sT1"][sl])
                W("vector", ["sT0", "sT1"][sl], P.cnt(["sT0", "sT1"][sl]))
                if tc == 0:
                    W("vector", "sG", ld_ffn)
                    W("vector", "sPE", conv_pe_done)
                    W("vector", "sYS0", P.cnt("sYS0"))
                    W("vector", "sYS1", P.cnt("sYS1"))
                v_op(lambda e, sl=sl, tc=tc: e.tensor_scalar(
                    xtn[:, sl * 1024:(sl + 1) * 1024],
                    ut2,
                    stats_sb[:, tc:tc + 1], stats_sb[:, 32 + tc:32 + tc + 1],
                    Alu.mult, Alu.add), drain=True)
                nrm_done[tc] = v_op(lambda e, sl=sl, tc=tc: e.tensor_tensor(
                    ysk[:, tc * 1024:(tc + 1) * 1024],
                    xtn[:, sl * 1024:(sl + 1) * 1024], skrep[:, :],
                    Alu.mult), inc=True, drain=True)
                for dc in range(8):
                    bi = 4 + ((tc * 8 + dc) % 4)
                    bank_wait(bi)
                    if dc == 0:
                        W("tensor", "sDV", nrm_done[tc])
                    pk = banks[bi]
                    last = mm(pk[:, 0:128],
                              xtn[:, sl * 1024 + dc * 128: sl * 1024 + (dc + 1) * 128],
                              id16[:, :], True, True, inc="sPE")
                    eng = "vector" if dc % 2 == 0 else "scalar"
                    W(eng, "sPE", last)
                    dst = xdt[:, dc * 4096 + tc * 128: dc * 4096 + (tc + 1) * 128]
                    if eng == "vector":
                        v_op(lambda e, dst=dst, pk=pk: e.tensor_copy(dst, pk[:, 0:128]), inc=True)
                        bank_guard[bi] = ("sDV", P.cnt("sDV"))
                    else:
                        sc_op(lambda e, dst=dst, pk=pk: e.copy(dst, pk[:, 0:128]), inc=True)
                        bank_guard[bi] = ("sSC", P.cnt("sSC"))
                xdt_done[tc] = ("sDV", P.cnt("sDV"), "sSC", P.cnt("sSC"))

            hl_done = {}
            for tc in range(32):
                bi = 6 + (tc % 2)
                bank_wait(bi)
                dvn, dvv, scn, scv = xdt_done[tc]
                W("tensor", dvn, dvv)
                W("tensor", scn, scv)
                pk = banks[bi]
                lastp = None
                for dc in range(8):
                    lhs = xdt[:, dc * 4096 + tc * 128: dc * 4096 + (tc + 1) * 128]
                    mm(pk[:, 0:128], lhs, at_sb[:, dc * 128:(dc + 1) * 128], dc == 0, dc == 7)
                for dc in range(8):
                    lhs = xdt[:, dc * 4096 + tc * 128: dc * 4096 + (tc + 1) * 128]
                    lastp = mm(pk[:, 128:136], lhs, rt_sb[:, dc * 8:(dc + 1) * 8],
                               dc == 0, dc == 7, inc="sPE" if dc == 7 else None)
                W("vector", "sPE", lastp)
                v_op(lambda e, tc=tc, pk=pk: e.tensor_copy(
                    h_sb[:, tc * 128:(tc + 1) * 128], pk[:, 0:128]))
                hl_done[tc] = v_op(lambda e, tc=tc, pk=pk: e.tensor_copy(
                    lg_sb[:, tc * 8:(tc + 1) * 8], pk[:, 128:136]), inc=True)
                bank_guard[bi] = ("sDV", hl_done[tc])
            P.emit("vector", lambda e: e.drain())

            lgv = lg_sb[:, :].rearrange("p (t e) -> p t e", e=8)
            W("vector", "sDV", hl_done[31])
            v_op(lambda e, lgv=lgv: e.reduce_max(gt1[:, :], lgv, axis=AxX, negate=True),
                 drain=True)
            v_op(lambda e, lgv=lgv: e.tensor_tensor(
                gt4.rearrange("p (t e) -> p t e", e=8), lgv,
                gt1[:, :].unsqueeze(2).broadcast_to([128, 32, 8]), Alu.add), drain=True)
            v_op(lambda e: e.tensor_scalar(gat[:, :], gt4[:, :], 0.0, None, Alu.is_ge),
                 drain=True)
            v_op(lambda e: e.scalar_tensor_tensor(
                gt3[:, :], gat[:, :], -BIG, lg_sb[:, :], Alu.mult, Alu.add), drain=True)
            v_op(lambda e: e.reduce_max(
                gt2[:, :], gt3[:, :].rearrange("p (t e) -> p t e", e=8), axis=AxX,
                negate=True), drain=True)
            v_op(lambda e: e.tensor_tensor(
                gt3[:, :].rearrange("p (t e) -> p t e", e=8),
                gt3[:, :].rearrange("p (t e) -> p t e", e=8),
                gt2[:, :].unsqueeze(2).broadcast_to([128, 32, 8]), Alu.add), drain=True)
            v_op(lambda e: e.tensor_scalar(gt3[:, :], gt3[:, :], 0.0, None, Alu.is_ge),
                 drain=True)
            mask_d = v_op(lambda e: e.tensor_tensor(gat[:, :], gat[:, :], gt3[:, :], Alu.add),
                          inc=True, drain=True)
            W("scalar", "sDV", mask_d)
            exp_d = sc_op(lambda e: e.activation(out=gt5[:, :], in_=gt4, func=Act.Exp),
                          inc=True)
            W("vector", "sSC", exp_d)
            v_op(lambda e: e.reduce_sum(
                gt1[:, :], gt5[:, :].rearrange("p (t e) -> p t e", e=8), axis=AxX), drain=True)
            v_op(lambda e: e.tensor_tensor(gat[:, :], gat[:, :], gt5[:, :], Alu.mult),
                 drain=True)
            v_op(lambda e: e.reduce_sum(
                gt2[:, :], gat[:, :].rearrange("p (t e) -> p t e", e=8), axis=AxX), drain=True)
            v_op(lambda e: e.scalar_tensor_tensor(
                gt2[:, :], gt1[:, :], EPS_GATE, gt2[:, :], Alu.mult, Alu.add), drain=True)
            v_op(lambda e: e.reciprocal(gt1[:, :], gt2[:, :]), drain=True)
            gat_d = v_op(lambda e: e.tensor_tensor(
                gat16[:, :].rearrange("p (t e) -> p t e", e=8),
                gat[:, :].rearrange("p (t e) -> p t e", e=8),
                gt1[:, :].unsqueeze(2).broadcast_to([128, 32, 8]),
                Alu.mult), inc=True, drain=True)
            gh_d = v_op(lambda e: e.tensor_tensor(
                h_sb[:, :].rearrange("p (t e r) -> p t e r", e=8, r=16),
                h_sb[:, :].rearrange("p (t e r) -> p t e r", e=8, r=16),
                gat16[:, :].rearrange("p (t e) -> p t e", e=8)
                .unsqueeze(3).broadcast_to([128, 32, 8, 16]),
                Alu.mult), inc=True, drain=True)

            ghT_d = {}
            for tc in range(32):
                bi = 4 + (tc % 2)
                bank_wait(bi)
                if tc == 0:
                    W("tensor", "sDV", gh_d)
                pk = banks[bi]
                last = mm(pk[:, 0:128], h_sb[:, tc * 128:(tc + 1) * 128], id16[:, :],
                          True, True, inc="sPE")
                W("vector", "sPE", last)
                ghT_d[tc] = v_op(lambda e, tc=tc, pk=pk: e.tensor_copy(
                    ghT[:, tc * 128:(tc + 1) * 128], pk[:, 0:128]), inc=True)
                bank_guard[bi] = ("sDV", ghT_d[tc])
            P.emit("vector", lambda e: e.drain())

            asm_d = {}
            store_cnt = {}
            yss = ["sY0", "sY1"]
            for tc in range(32):
                bi0 = (tc % 2) * 2
                bank_wait(bi0)
                bank_wait(bi0 + 1)
                if tc == 0:
                    W("tensor", "sDV", ghT_d[31])
                pA, pB = banks[bi0], banks[bi0 + 1]
                for dc in range(8):
                    lhs = xdt[:, dc * 4096 + tc * 128: dc * 4096 + (tc + 1) * 128]
                    mm(pA[:, 0:512], lhs, wffn_sb[:, dc * 1024:dc * 1024 + 512],
                       dc == 0, False)
                    mm(pB[:, 0:512], lhs, wffn_sb[:, dc * 1024 + 512:(dc + 1) * 1024],
                       dc == 0, False)
                mm(pA[:, 0:512], ones1[0:1, :], bias_sb[0:1, 0:512], False, False)
                mm(pB[:, 0:512], ones1[0:1, :], bias_sb[0:1, 512:1024], False, False)
                mm(pA[:, 0:512], ghT[:, tc * 128:(tc + 1) * 128], bt2_sb[:, 0:512],
                   False, True)
                lastp = mm(pB[:, 0:512], ghT[:, tc * 128:(tc + 1) * 128],
                           bt2_sb[:, 512:1024], False, True, inc="sPE")
                sl = tc % 2
                W("gpsimd", "sA2A", 1)
                if tc >= 2:
                    W("gpsimd", "sDV", asm_d[tc - 2])
                g_dma(y_sb[:, sl * 1024:(sl + 1) * 1024].rearrange(
                    "p (j d) -> p j d", j=8),
                    d_a2aout[:, tc * 128:(tc + 1) * 128, :].rearrange("j t d -> t j d"),
                    yss[sl])
                W("vector", yss[sl], P.cnt(yss[sl]))
                v_op(lambda e, sl=sl, tc=tc: e.tensor_tensor(
                    ypl[:, sl * 1024:(sl + 1) * 1024],
                    y_sb[:, sl * 1024:(sl + 1) * 1024],
                    ysk[:, tc * 1024:(tc + 1) * 1024], Alu.add), drain=True)
                if tc >= 2:
                    W("vector", ("sST", "sST1")[tc % 2], store_cnt[tc - 2])
                W("vector", "sPE", lastp)
                v_op(lambda e, sl=sl, pA=pA: e.tensor_tensor(
                    osb[:, sl * 1024:sl * 1024 + 512], pA[:, 0:512],
                    ypl[:, sl * 1024:sl * 1024 + 512], Alu.add))
                ev = v_op(lambda e, sl=sl, pB=pB: e.tensor_tensor(
                    osb[:, sl * 1024 + 512:sl * 1024 + 1024], pB[:, 0:512],
                    ypl[:, sl * 1024 + 512:sl * 1024 + 1024], Alu.add), inc=True,
                    drain=True)
                asm_d[tc] = ev
                bank_guard[bi0] = ("sDV", ev)
                bank_guard[bi0 + 1] = ("sDV", ev)
                W("gpsimd", "sDV", ev)
                accsem = (sAC0, sAC1)[sl]
                P.emit("gpsimd", lambda e, sl=sl, tc=tc, accsem=accsem: e.dma_start(
                    out=osb[:, sl * 1024:(sl + 1) * 1024],
                    in_=d_utok[tc * 128:(tc + 1) * 128, :],
                    accum_op=Alu.add).then_inc(accsem, 16))
                acc = P.inc("sAC%d" % sl, 16)
                W("sync", "sAC%d" % sl, acc)
                store_cnt[tc] = s_dma(d_out[tc * 128:(tc + 1) * 128, :],
                                      osb[:, sl * 1024:(sl + 1) * 1024],
                                      ("sST", "sST1")[tc % 2])
            W("sync", "sST", P.cnt("sST"))
            W("sync", "sST1", P.cnt("sST1"))

        if trunc is not None:
            cut = P.marks[trunc]
            for e2 in P.ops:
                P.ops[e2] = P.ops[e2][:cut[e2]]

        @block.sync
        def _(e):
            for f in P.ops["sync"]:
                f(e)

        @block.gpsimd
        def _(e):
            for f in P.ops["gpsimd"]:
                f(e)

        @block.tensor
        def _(e):
            for f in P.ops["tensor"]:
                f(e)

        @block.vector
        def _(e):
            for f in P.ops["vector"]:
                f(e)

        @block.scalar
        def _(e):
            for f in P.ops["scalar"]:
                f(e)

    return nc


def kernel(L0_attn_u, L0_attn_k, L0_attn_skip, L0_ffn_W, L0_ffn_b, L0_ffn_R,
           L0_ffn_A, L0_ffn_B):
    u = np.ascontiguousarray(np.asarray(L0_attn_u, dtype=np.float32))
    kk = np.ascontiguousarray(np.asarray(L0_attn_k, dtype=np.float32))
    skip = np.asarray(L0_attn_skip, dtype=np.float32)
    Wf = np.asarray(L0_ffn_W, dtype=np.float32)
    bf = np.asarray(L0_ffn_b, dtype=np.float32)
    Rf = np.asarray(L0_ffn_R, dtype=np.float32)
    Af = np.asarray(L0_ffn_A, dtype=np.float32)
    Bf = np.asarray(L0_ffn_B, dtype=np.float32)

    fc, fpc, wt, st, ident, fs1 = _consts()
    w_do = np.ascontiguousarray(Wf.T)
    rt = np.ascontiguousarray(Rf.T)
    at = np.ascontiguousarray(Af.transpose(2, 0, 1).reshape(D, 128))
    bt2 = np.ascontiguousarray(Bf.transpose(0, 2, 1).reshape(128, D))
    bias_row = np.ascontiguousarray(bf.reshape(1, D))
    skiprep = np.ascontiguousarray(np.broadcast_to(skip.reshape(1, D), (128, D)))

    in_maps = []
    for c in range(NC):
        d0 = c * DL
        bc, hc = c // 2, c % 2
        ucv = u[:, :, d0:d0 + DL].reshape(B, 64, 128, DL)
        in_maps.append({
            "u_conv": np.ascontiguousarray(ucv),
            "u_tok": np.ascontiguousarray(u[bc, hc * NT:(hc + 1) * NT, :]),
            "k_conv": np.ascontiguousarray(
                kk[d0:d0 + DL, :].reshape(2, 64, 64, 128).transpose(0, 2, 3, 1)),
            "fc": fc, "fs1": fs1, "fpc": fpc, "wt": wt, "st": st, "identf": ident,
            "wffn": w_do, "biasv": bias_row, "skiprep": skiprep,
            "rt": rt, "at": at, "bt2": bt2,
        })

    nc = build_nc()
    with allow_ici_all_to_all():
        res = run_bass_kernel_spmd(nc, in_maps, core_ids=list(range(NC)))

    out = np.empty((B, L, D), np.float32)
    for c in range(NC):
        bc, hc = c // 2, c % 2
        out[bc, hc * NT:(hc + 1) * NT, :] = res.results[c]["out"]
    return out


if __name__ == "__main__":
    import importlib.util
    spec = importlib.util.spec_from_file_location("_ref", "/root/problem/reference.py")
    _r = importlib.util.module_from_spec(spec)
    spec.loader.exec_module(_r)
    inp = _r.setup_inputs()
    o = kernel(**{k: np.asarray(v) for k, v in inp.items()})
    ref = np.asarray(_r.reference(**inp))
    err = o - ref
    print("rel L2:", np.sqrt((err ** 2).mean() / (ref ** 2).mean()),
          "max abs:", np.abs(err).max())
